# version 05
# baseline (speedup 1.0000x reference)
import numpy as np
from contextlib import ExitStack

import concourse.bass as bass
import concourse.mybir as mybir
from concourse.bass_utils import run_bass_kernel_spmd

F32 = mybir.dt.float32
BF16 = mybir.dt.bfloat16
I32 = mybir.dt.int32
AF = mybir.ActivationFunctionType
ALU = mybir.AluOpType

BATCH, SEQ, D = 2, 8192, 4096
H, HD, BS = 32, 128, 256
LAYER_IDX, N_LAYERS = 1, 32
EPS = 1e-5
INTER, RANK = 11008, 1024

NCORES = 8
T = 2048
NB = T // BS
NFI = D // 128
NTC = T // 128
GCH = INTER // 128
FCH = 2 * GCH

TAPS = []
STOP_AFTER = None


def _decays_np():
    base = 1.0 / 2.0 ** (8.0 / H)
    exponent = np.arange(H, dtype=np.float64) + 1.0
    factor = 1.0 - LAYER_IDX / (N_LAYERS - 1 + 1e-5) + 1e-5
    slope = (base ** exponent) * factor
    r = np.arange(BS, dtype=np.float64) + 1.0
    q_decay = np.exp(-slope[:, None] * r[None, :])
    k_decay = np.exp(-slope[:, None] * (BS - r)[None, :])
    diff = r[:, None] - r[None, :]
    diag = slope[:, None, None] * diff[None, :, :]
    diag_decay = np.where(diag >= 0, np.exp(-diag), 0.0)
    block_decay = np.exp(-slope * BS)
    return slope, q_decay, k_decay, diag_decay, block_decay


def _img(a2d):
    rows, cols = a2d.shape
    c = rows // 128
    return np.ascontiguousarray(
        a2d.reshape(c, 128, cols).transpose(1, 0, 2).reshape(128, c * cols))


def pack_inputs(inp):
    x = np.asarray(inp["L0_attn_hidden_states"], np.float32)
    qkv_w = np.asarray(inp["L0_attn_qkv_w"], np.float32)
    out_w = np.asarray(inp["L0_attn_out_w"], np.float32)
    gate_w = np.asarray(inp["L0_attn_gate_w"], np.float32)
    norm_w = np.asarray(inp["L0_attn_norm_w"], np.float32)
    U1 = np.asarray(inp["L0_ffn_U1"], np.float32)
    V1 = np.asarray(inp["L0_ffn_V1"], np.float32)
    U2 = np.asarray(inp["L0_ffn_U2"], np.float32)
    V2 = np.asarray(inp["L0_ffn_V2"], np.float32)
    b1 = np.asarray(inp["L0_ffn_b1"], np.float32)
    b2 = np.asarray(inp["L0_ffn_b2"], np.float32)

    qkvT = np.ascontiguousarray(qkv_w.T)
    gateT = np.ascontiguousarray(gate_w.T)
    outT = np.ascontiguousarray(out_w.T)

    qk_cols = []
    for h in range(H):
        qk_cols.append(qkvT[:, 384 * h:384 * h + 128])
        qk_cols.append(qkvT[:, 384 * h + 128:384 * h + 256])
    wqk = np.concatenate([_img(c) for c in qk_cols], axis=1)

    v_cols = []
    for pan in range(16):
        h0, h1 = 2 * pan, 2 * pan + 1
        v_cols.append(np.concatenate(
            [qkvT[:, 384 * h0 + 256:384 * h0 + 384],
             qkvT[:, 384 * h1 + 256:384 * h1 + 384]], axis=1))
    wv = np.concatenate([_img(c) for c in v_cols], axis=1)

    wg = np.concatenate([_img(gateT[:, 256 * s:256 * (s + 1)]) for s in range(16)], axis=1)
    wo = np.concatenate([_img(outT[:, 256 * s:256 * (s + 1)]) for s in range(16)], axis=1)

    u1 = np.concatenate([_img(U1[:, 128 * rc:128 * (rc + 1)]) for rc in range(8)], axis=1)

    v1_order = []
    for g in range(GCH):
        v1_order += [g, GCH + g]
    v1 = np.concatenate([_img(V1[:, 128 * fc:128 * (fc + 1)]) for fc in v1_order], axis=1)

    u2 = np.ascontiguousarray(
        U2.reshape(GCH, 128, RANK).transpose(1, 0, 2).reshape(128, GCH * RANK))

    v2 = np.concatenate([_img(V2[:, 512 * s:512 * (s + 1)]) for s in range(8)], axis=1)

    slope, q_decay, k_decay, diag_decay, block_decay = _decays_np()
    qdm = np.zeros((128, H * 768), np.float32)
    for h in range(H):
        for jc in range(2):
            qdm[:, h * 768 + jc * 256: h * 768 + (jc + 1) * 256] = \
                diag_decay[h, :, jc * 128:(jc + 1) * 128].T
        qdm[:, h * 768 + 512: h * 768 + 768] = q_decay[h][None, :]
    kdec = np.zeros((128, H * 2), np.float32)
    for h in range(H):
        for jc in range(2):
            kdec[:, h * 2 + jc] = k_decay[h, jc * 128:(jc + 1) * 128]
    bdv = np.tile(block_decay.reshape(1, H), (128, 1)).astype(np.float32)

    ccw_all = []
    for core in range(NCORES):
        g = core % 4
        w = np.zeros((H, 4), np.float64)
        for r in range(g):
            w[:, r] = block_decay ** (8.0 * (g - 1 - r))
        ccw_all.append(np.tile(w.reshape(1, H * 4), (128, 1)).astype(np.float32))

    nwc = np.ascontiguousarray(norm_w.reshape(NFI, 128).T)
    b1c = np.stack([b1[128 * fc:128 * (fc + 1)] for fc in v1_order], axis=1)
    b2_pack = b2.reshape(1, D)

    shared = dict(wqk=wqk, wv=wv, wg=wg, wo=wo, u1=u1, v1=v1, u2=u2, v2=v2,
                  qdm=qdm, kdec=kdec, bdv=bdv, nwc=nwc,
                  b1c=np.ascontiguousarray(b1c),
                  b2=np.ascontiguousarray(b2_pack))

    xr = x.reshape(BATCH * SEQ, D)
    in_maps = []
    for core in range(NCORES):
        sl = xr[core * T:(core + 1) * T]
        m = dict(shared)
        m["x_tm"] = np.ascontiguousarray(sl)
        m["x_fm"] = _img(np.ascontiguousarray(sl.T))
        m["ccw"] = ccw_all[core]
        in_maps.append(m)
    return in_maps



ENGINES = ("sync", "scalar", "vector", "tensor", "gpsimd")


class Prog:
    def __init__(self, nc, es):
        self.nc, self.es = nc, es
        self.ops = {e: [] for e in ENGINES}
        self.sem, self.cnt = {}, {}
        self.keys = {}

    def new_sem(self, name):
        self.sem[name] = self.es.enter_context(self.nc.semaphore(name))
        self.cnt[name] = 0

    def emit(self, eng, fn):
        self.ops[eng].append(fn)

    def wait(self, eng, name, val):
        if val is None or val <= 0:
            return
        h = self.sem[name]
        self.emit(eng, lambda e, h=h, v=val: e.wait_ge(h, v))

    def wait_key(self, eng, key):
        g = self.keys.get(key)
        if g is not None:
            self.wait(eng, g[0], g[1])

    def set_key(self, key, sem, cnt):
        self.keys[key] = (sem, cnt)

    def seed_key(self, key, sem):
        self.keys[key] = (sem, self.cnt[sem])

    def op(self, eng, build, sem=None, amt=1):
        if sem:
            c = self.cnt[sem] = self.cnt[sem] + amt
            h = self.sem[sem]
            self.emit(eng, lambda e, b=build, h=h, a=amt: b(e).then_inc(h, a))
            return c
        self.emit(eng, lambda e, b=build: b(e))
        return None

    def barrier(self):
        for s in ["ST", "SS0", "SS1", "SS2", "SS3"]:
            self.wait("sync", s, self.cnt[s])
        c = self.cnt["FG"] = self.cnt["FG"] + 1
        h = self.sem["FG"]
        self.emit("sync", lambda e, h=h: e.sem_inc(h, 1))
        return c

    def dma(self, eng, out, in_, sem, **kw):
        c = self.cnt[sem] = self.cnt[sem] + 16
        h = self.sem[sem]
        self.emit(eng, lambda e, o=out, i=in_, h=h, kw=kw:
                  e.dma_start(out=o, in_=i, **kw).then_inc(h, 16))
        return c


def build_program(taps=(), stop_after=None):
    nc = bass.Bass("TRN2", target_bir_lowering=False, debug=False)

    ext = {}
    for name, shape in [
        ("x_tm", (T, D)), ("x_fm", (128, NFI * T)),
        ("wqk", (128, 64 * NFI * 128)), ("wv", (128, 16 * NFI * 256)),
        ("wg", (128, 16 * NFI * 256)), ("wo", (128, 16 * NFI * 256)),
        ("u1", (128, 8 * NFI * 128)), ("v1", (128, FCH * 8 * 128)),
        ("u2", (128, GCH * RANK)), ("v2", (128, 8 * 8 * 512)),
        ("qdm", (128, H * 768)),
        ("kdec", (128, H * 2)), ("bdv", (128, H)), ("ccw", (128, H * 4)),
        ("nwc", (128, NFI)), ("b1c", (128, FCH)), ("b2", (1, D)),
    ]:
        ext[name] = nc.dram_tensor(name, list(shape), F32, kind="ExternalInput").ap()
    OUT = nc.dram_tensor("out", [T, D], F32, kind="ExternalOutput").ap()

    scr = {}
    scr["q_scr"] = nc.dram_tensor("q_scr", [H, 128, T], BF16).ap()
    scr["k_scr"] = nc.dram_tensor("k_scr", [H, 128, T], BF16).ap()
    scr["v_scr"] = nc.dram_tensor("v_scr", [16, NTC, 128, 256], BF16).ap()
    scr["kd_scr"] = nc.dram_tensor("kd_scr", [H, NB, 128, 256], BF16).ap()
    scr["g_scr"] = nc.dram_tensor("g_scr", [NTC, 128, D], BF16).ap()
    scr["z_scr"] = nc.dram_tensor("z_scr", [128, NTC * D], BF16).ap()
    scr["pl1_scr"] = nc.dram_tensor("pl1_scr", [NTC, 128, D], F32).ap()
    scr["x1_scr"] = nc.dram_tensor("x1_scr", [NTC, 128, D], F32).ap()
    scr["x1fm_scr"] = nc.dram_tensor("x1fm_scr", [128, NTC * D], BF16).ap()
    scr["qdm_scr"] = nc.dram_tensor("qdm_scr", [128, H * 768], BF16).ap()
    scr["pl2_scr"] = nc.dram_tensor("pl2_scr", [NTC, 128, D], F32).ap()
    cc_in_t = nc.dram_tensor("cc_in", [H * 128, 128], F32)
    cc_out_t = nc.dram_tensor("cc_out", [4 * H * 128, 128], F32)

    tapped = {}
    for name in taps:
        if name == "S_dump":
            tapped[name] = nc.dram_tensor("tap_S_dump", [128, H * 128], F32,
                                          kind="ExternalOutput").ap()
            continue
        s = scr[name]
        tapped[name] = nc.dram_tensor("tap_" + name, list(s.shape), s.dtype,
                                      kind="ExternalOutput").ap()

    es = ExitStack()
    with es:
        t = {"ext": ext, "OUT": OUT, "scr": scr, "cc_in": cc_in_t,
             "cc_out": cc_out_t, "tapped": tapped}
        sb = lambda n, shape, dt: es.enter_context(nc.sbuf_tensor(n, list(shape), dt))
        t["PX"] = sb("PX", (128, 65536), BF16)
        t["WS"] = [sb("WS0", (128, 8192), BF16), sb("WS1", (128, 8192), BF16)]
        t["STG"] = sb("STG", (128, 4096), BF16)
        t["CST"] = sb("CST", (128, 3584), BF16)
        t["kdec"] = sb("kdecs", (128, H * 2), F32)
        t["bdv"] = sb("bdvs", (128, H), F32)
        t["ccw"] = sb("ccws", (128, H * 4), F32)
        t["nwc"] = sb("nwcs", (128, NFI), F32)
        t["b1c"] = sb("b1cs", (128, FCH), F32)
        t["b2"] = sb("b2s", (1, 1024), BF16)
        t["ones"] = sb("oness", (1, 512), BF16)
        t["epsc"] = sb("epscs", (128, 1), F32)
        t["ident"] = sb("idents", (128, 128), BF16)
        t["ident_i"] = sb("identis", (128, 128), I32)
        t["S"] = sb("Ss", (128, H * 128), F32)
        t["lnt"] = sb("lnts", (128, 64), F32)
        t["banks"] = [es.enter_context(nc.psum_tensor(f"bk{i}", [128, 512], F32))
                      for i in range(8)]

        p = Prog(nc, es)
        for s in ["LD", "WL0", "WL1", "PE", "DV", "SC", "GP", "ST",
                  "SS0", "SS1", "SS2", "SS3", "FG", "B20", "B21",
                  "QL0", "QL1", "GL0", "GL1", "XL0", "XL1", "CC"]:
            p.new_sem(s)

        phases = [emit_consts, emit_A1, emit_A3, emit_A2, emit_sstart,
                  emit_B, emit_C, emit_LN1, emit_D1, emit_D23, emit_D4]
        for ph in phases:
            ph(p, t)
            if stop_after is not None and ph.__name__ == stop_after:
                break
        emit_taps(p, t)
        emit_final(p, t)

        block = es.enter_context(nc.Block())

        @block.sync
        def _(e):
            for f in p.ops["sync"]:
                f(e)

        @block.scalar
        def _(e):
            for f in p.ops["scalar"]:
                f(e)

        @block.vector
        def _(e):
            for f in p.ops["vector"]:
                f(e)

        @block.tensor
        def _(e):
            for f in p.ops["tensor"]:
                f(e)

        @block.gpsimd
        def _(e):
            for f in p.ops["gpsimd"]:
                f(e)

    return nc



def emit_consts(p, t):
    ext = t["ext"]
    for name, dst in [("kdec", t["kdec"]), ("bdv", t["bdv"]), ("ccw", t["ccw"]),
                      ("nwc", t["nwc"]), ("b1c", t["b1c"])]:
        p.dma("gpsimd", dst[:, :], ext[name], "LD", max_dma_last_dim=8192)
    p.dma("gpsimd", t["scr"]["qdm_scr"], ext["qdm"], "LD", max_dma_last_dim=8192)
    PX = t["PX"]
    for i in range(4):
        w = NFI * T // 4
        p.dma("gpsimd", PX[:, i * w:(i + 1) * w], ext["x_fm"][:, i * w:(i + 1) * w],
              "LD", max_dma_last_dim=8192)
    p.op("gpsimd", lambda e: e.iota(t["ident_i"][:, :], pattern=[[1, 128]],
                                    base=0, channel_multiplier=-1), sem="GP")
    p.op("gpsimd", lambda e: e.memset(t["ones"][:, :], 1.0), sem="GP")
    p.op("gpsimd", lambda e: e.memset(t["epsc"][:, :], EPS), sem="GP")
    p.op("gpsimd", lambda e: e.memset(t["S"][:, :], 0.0), sem="GP")
    gp = p.cnt["GP"]
    p.wait("vector", "GP", gp)
    c = p.op("vector", lambda e: e.tensor_scalar(t["ident"][:, :], t["ident_i"][:, :],
                                                 0.0, None, op0=ALU.is_equal), sem="DV")
    p.set_key("ident", "DV", c)
    p.set_key("consts", "LD", p.cnt["LD"])
    p.set_key("zeroS", "GP", gp)


def _slab_loader(p, t):
    WS = t["WS"]
    state = {"idx": 0, "free_pe": []}

    def load(src_ap, ncols):
        i = state["idx"]
        slot = i % 2
        sem = "WL0" if slot == 0 else "WL1"
        if i >= 2:
            p.wait("gpsimd", "PE", state["free_pe"][i - 2])
        c = p.dma("gpsimd", WS[slot][:, 0:ncols], src_ap, sem, max_dma_last_dim=8192)
        state["idx"] = i + 1
        return WS[slot], sem, c

    def free():
        state["free_pe"].append(p.cnt["PE"])

    return load, free



def emit_A1(p, t):
    ext, PX, STG, banks = t["ext"], t["PX"], t["STG"], t["banks"]
    load, free = _slab_loader(p, t)
    p._slab_load, p._slab_free = load, free

    for s in range(64):
        src = ext["wqk"][:, s * NFI * 128:(s + 1) * NFI * 128]
        slot, lsem, lcnt = load(src, NFI * 128)
        p.wait("tensor", lsem, lcnt)
        if s == 0:
            p.wait_key("tensor", "consts")
        h, isq = s // 2, (s % 2 == 0)
        for tt in range(4):
            b = (s * 4 + tt) % 8
            p.wait_key("tensor", ("bank", b))
            for c in range(NFI):
                p.op("tensor", lambda e, o=banks[b][:, :],
                     l=slot[:, c * 128:(c + 1) * 128],
                     r=PX[:, c * T + tt * 512: c * T + tt * 512 + 512],
                     st=(c == 0), sp=(c == NFI - 1):
                     e.matmul(o, l, r, start=st, stop=sp),
                     sem="PE" if c == NFI - 1 else None)
            pe_c = p.cnt["PE"]
            stg = STG[:, (tt % 4) * 512:(tt % 4) * 512 + 512]
            p.wait("scalar", "PE", pe_c)
            p.wait_key("scalar", ("A1stg", tt % 4))
            sc_c = p.op("scalar", lambda e, o=stg, i=banks[b][:, :]:
                        e.activation(o, i, AF.Silu), sem="SC")
            p.set_key(("bank", b), "SC", sc_c)
            p.wait("sync", "SC", sc_c)
            dst = (t["scr"]["q_scr"] if isq else t["scr"]["k_scr"])[h, :, tt * 512:(tt + 1) * 512]
            st_c = p.dma("sync", dst, stg, f"SS{tt % 4}")
            p.set_key(("A1stg", tt % 4), f"SS{tt % 4}", st_c)
        free()

    for pan in range(16):
        src = ext["wv"][:, pan * NFI * 256:(pan + 1) * NFI * 256]
        slot, lsem, lcnt = load(src, NFI * 256)
        p.wait("tensor", lsem, lcnt)
        for tc in range(NTC):
            b = tc % 8
            p.wait_key("tensor", ("bank", b))
            for c in range(NFI):
                p.op("tensor", lambda e, o=banks[b][:, 0:256],
                     l=PX[:, c * T + tc * 128: c * T + tc * 128 + 128],
                     r=slot[:, c * 256:(c + 1) * 256],
                     st=(c == 0), sp=(c == NFI - 1):
                     e.matmul(o, l, r, start=st, stop=sp),
                     sem="PE" if c == NFI - 1 else None)
            pe_c = p.cnt["PE"]
            stg = STG[:, 2048 + (tc % 4) * 256: 2048 + (tc % 4) * 256 + 256]
            p.wait("scalar", "PE", pe_c)
            p.wait_key("scalar", ("Avstg", tc % 4))
            sc_c = p.op("scalar", lambda e, o=stg, i=banks[b][:, 0:256]:
                        e.activation(o, i, AF.Silu), sem="SC")
            p.set_key(("bank", b), "SC", sc_c)
            p.wait("sync", "SC", sc_c)
            st_c = p.dma("sync", t["scr"]["v_scr"][pan, tc, :, :], stg, f"SS{tc % 4}")
            p.set_key(("Avstg", tc % 4), f"SS{tc % 4}", st_c)
        free()

    p.set_key("A1_stores", "FG", p.barrier())



def emit_A2(p, t):
    banks, CST, STG, S = t["banks"], t["CST"], t["STG"], t["S"]
    kdec, bdv, ident = t["kdec"], t["bdv"], t["ident"]
    kscr, vscr, kdscr = t["scr"]["k_scr"], t["scr"]["v_scr"], t["scr"]["kd_scr"]

    p.wait_key("sync", "A1_stores")
    p.wait_key("tensor", "ident")
    p.wait_key("vector", "zeroS")
    for par in range(2):
        p.seed_key(("A2kdstg", par), f"SS{2 + par}")
        p.seed_key(("A2bank", par), "SC")
        p.seed_key(("A2cb", par), "SC")

    for b in range(NB):
        for h in range(H):
            par = h % 2
            qsem = "QL0" if par == 0 else "QL1"
            slot = CST[:, par * 1792:par * 1792 + 512]
            p.wait_key("sync", ("A2slot", par))
            p.dma("sync", slot[:, 0:256], kscr[h, :, b * 256:(b + 1) * 256], qsem)
            pan, col = h // 2, (h % 2) * 128
            p.dma("sync", slot[:, 256:384], vscr[pan, 2 * b, :, col:col + 128], qsem)
            lcnt = p.dma("sync", slot[:, 384:512], vscr[pan, 2 * b + 1, :, col:col + 128], qsem)

            tb = banks[par].ap().bitcast(BF16)
            p.wait("tensor", qsem, lcnt)
            p.wait_key("tensor", ("A2bank", par))
            p.op("tensor", lambda e, o=tb[:, 0:128], i=slot[:, 0:128], idt=ident[:, :]:
                 e.transpose(o, i, idt))
            pe_tr = p.op("tensor", lambda e, o=tb[:, 128:256], i=slot[:, 128:256],
                         idt=ident[:, :]: e.transpose(o, i, idt), sem="PE")

            kdt = STG[:, par * 512:par * 512 + 256]
            p.wait("vector", "PE", pe_tr)
            p.wait_key("vector", ("A2kdstg", par))
            p.op("vector", lambda e, o=kdt[:, 0:128], i=tb[:, 0:128],
                 s=kdec[:, 2 * h:2 * h + 1]:
                 e.tensor_scalar(o, i, s, None, op0=ALU.mult))
            dv1 = p.op("vector", lambda e, o=kdt[:, 128:256], i=tb[:, 128:256],
                       s=kdec[:, 2 * h + 1:2 * h + 2]:
                       e.tensor_scalar(o, i, s, None, op0=ALU.mult), sem="DV")
            p.set_key(("A2bank", par), "DV", dv1)
            p.wait("sync", "DV", dv1)
            st_c = p.dma("sync", kdscr[h, b, :, :], kdt, f"SS{2 + par}")

            cb = banks[2 + par]
            p.wait("tensor", "DV", dv1)
            p.wait_key("tensor", ("A2cb", par))
            p.op("tensor", lambda e, o=cb[:, 0:128], l=kdt[:, 0:128], r=slot[:, 256:384]:
                 e.matmul(o, l, r, start=True, stop=False))
            pe_c = p.op("tensor", lambda e, o=cb[:, 0:128], l=kdt[:, 128:256],
                        r=slot[:, 384:512]:
                        e.matmul(o, l, r, start=False, stop=True), sem="PE")
            p.set_key(("A2slot", par), "PE", pe_c)
            p.set_key(("A2kdstg", par), f"SS{2 + par}", st_c)

            p.wait("vector", "PE", pe_c)
            p.wait_key("vector", ("Schain", h))
            dvs = p.op("vector", lambda e, o=S[:, h * 128:(h + 1) * 128],
                       s=bdv[:, h:h + 1], c=cb[:, 0:128]:
                       e.scalar_tensor_tensor(o, o, s, c, op0=ALU.mult, op1=ALU.add),
                       sem="DV")
            p.set_key(("Schain", h), "DV", dvs)
            p.set_key(("A2cb", par), "DV", dvs)

    p.wait("sync", "DV", p.cnt["DV"])
    src = S[:, :].rearrange("d (h e) -> d h e", h=H)
    dst = t["cc_in"].ap().rearrange("(h d) e -> d h e", h=H)
    p.dma("sync", dst, src, "ST")
    ccflag = p.barrier()
    p.wait("gpsimd", "FG", ccflag)
    p.op("gpsimd", lambda e: e.collective_compute(
        "AllGather", ALU.bypass, replica_groups=[[0, 1, 2, 3], [4, 5, 6, 7]],
        ins=[t["cc_in"].ap().opt()], outs=[t["cc_out"].ap().opt()]), sem="CC")
    p.set_key("A2_done_PE", "PE", p.cnt["PE"])
    p.set_key("A2_stores", "FG", ccflag)



def emit_A3(p, t):
    ext, PX, STG, banks = t["ext"], t["PX"], t["STG"], t["banks"]
    load, free = p._slab_load, p._slab_free
    for par in range(2):
        p.seed_key(("A3stg", par), f"SS{par}")
    for pan in range(16):
        src = ext["wg"][:, pan * NFI * 256:(pan + 1) * NFI * 256]
        slot, lsem, lcnt = load(src, NFI * 256)
        p.wait("tensor", lsem, lcnt)
        for tc in range(NTC):
            b = 4 + tc % 4
            p.wait_key("tensor", ("bank", b))
            for c in range(NFI):
                p.op("tensor", lambda e, o=banks[b][:, 0:256],
                     l=PX[:, c * T + tc * 128: c * T + tc * 128 + 128],
                     r=slot[:, c * 256:(c + 1) * 256],
                     st=(c == 0), sp=(c == NFI - 1):
                     e.matmul(o, l, r, start=st, stop=sp),
                     sem="PE" if c == NFI - 1 else None)
            pe_c = p.cnt["PE"]
            stg = STG[:, 2048 + (tc % 2) * 256: 2048 + (tc % 2) * 256 + 256]
            p.wait("scalar", "PE", pe_c)
            p.wait_key("scalar", ("A3stg", tc % 2))
            sc_c = p.op("scalar", lambda e, o=stg, i=banks[b][:, 0:256]:
                        e.activation(o, i, AF.Sigmoid), sem="SC")
            p.set_key(("bank", b), "SC", sc_c)
            p.wait("sync", "SC", sc_c)
            st_c = p.dma("sync", t["scr"]["g_scr"][tc, :, pan * 256:(pan + 1) * 256], stg, f"SS{tc % 2}")
            p.set_key(("A3stg", tc % 2), f"SS{tc % 2}", st_c)
        free()
    p.set_key("A3_stores", "FG", p.barrier())



def emit_sstart(p, t):
    CST, S, ccw = t["CST"], t["S"], t["ccw"]
    cc_out = t["cc_out"].ap()
    p.wait("scalar", "CC", p.cnt["CC"])
    p.wait_key("scalar", "A2_done_PE")
    aggv = CST[:, :].bitcast(F32)
    for h in range(H):
        par = h % 2
        agg = aggv[:, par * 512:par * 512 + 512]
        sem = "GL0" if par == 0 else "GL1"
        p.wait_key("scalar", ("aggslot", par))
        lc = None
        for r in range(4):
            lc = p.dma("scalar", agg[:, r * 128:(r + 1) * 128],
                       cc_out[r * H * 128 + h * 128: r * H * 128 + (h + 1) * 128, :], sem)
        p.wait("vector", sem, lc)
        p.wait_key("vector", ("Schain", h))
        dv = None
        for r in range(4):
            if dv is not None:
                p.wait("vector", "DV", dv)
            if r == 0:
                dv = p.op("vector", lambda e, o=S[:, h * 128:(h + 1) * 128],
                          i=agg[:, 0:128], s=ccw[:, h * 4:h * 4 + 1]:
                          e.tensor_scalar(o, i, s, None, op0=ALU.mult), sem="DV")
            else:
                dv = p.op("vector", lambda e, o=S[:, h * 128:(h + 1) * 128],
                          i=agg[:, r * 128:(r + 1) * 128],
                          s=ccw[:, h * 4 + r:h * 4 + r + 1],
                          m=S[:, h * 128:(h + 1) * 128]:
                          e.scalar_tensor_tensor(o, i, s, m, op0=ALU.mult, op1=ALU.add),
                          sem="DV")
        p.set_key(("aggslot", par), "DV", dv)
        p.set_key(("Schain", h), "DV", dv)
    if "S_dump" in t["tapped"]:
        p.wait("sync", "DV", p.cnt["DV"])
        p.dma("sync", t["tapped"]["S_dump"], t["S"][:, :], "ST")



def emit_B(p, t):
    banks, CST, STG, PX, S = t["banks"], t["CST"], t["STG"], t["PX"], t["S"]
    bdv, ident, lnt = (t[k] for k in ["bdv", "ident", "lnt"])
    qscr, kscr, vscr, kdscr = (t["scr"][n] for n in ["q_scr", "k_scr", "v_scr", "kd_scr"])
    gscr, zscr, mscr = t["scr"]["g_scr"], t["scr"]["z_scr"], t["scr"]["qdm_scr"]

    attn = PX[:, 0:8192]
    gate_slot = [PX[:, 8192:16384], PX[:, 16384:24576]]
    zbf = PX[:, 40960:49152]

    p.wait_key("sync", "A3_stores")
    p.wait_key("sync", "A2_stores")
    p.seed_key("BattnSC", "PE")
    p.seed_key("BattnDV", "PE")
    for ic in range(2):
        p.seed_key(("Bzb", ic), "PE")
    for par in range(2):
        p.seed_key(("Bgate", par), "PE")
        p.seed_key(("BbankS", par), "SC")
        p.seed_key(("BbankP", par), "SC")
        p.seed_key(("BbankC", par), "DV")
        p.seed_key(("Bztb", par), "DV")
        p.seed_key(("Bslot", par), "PE")
        p.seed_key(("Bsct", par), "PE")
        p.seed_key(("Bsd", par), "PE")
        p.seed_key(("Bqd", par), "PE")
        p.seed_key(("Bsbf", par), "PE")
    p.seed_key("Baggdone", "DV")
    for i in range(2):
        p.seed_key(("Bzstg", i), f"SS{i}")

    def load_head(b, h):
        par = (b * H + h) % 2
        qsem = "QL0" if par == 0 else "QL1"
        slot = CST[:, par * 1792:par * 1792 + 1792]
        p.wait_key("sync", ("Bslot", par))
        if b == 0 and h < 2:
            kk = p.keys.get("Baggdone")
            p.wait("sync", kk[0], kk[1])
        p.dma("sync", slot[:, 0:256], qscr[h, :, b * 256:(b + 1) * 256], qsem)
        p.dma("sync", slot[:, 256:512], kscr[h, :, b * 256:(b + 1) * 256], qsem)
        p.dma("sync", slot[:, 512:768], kdscr[h, b, :, :], qsem)
        pan, col = h // 2, (h % 2) * 128
        p.dma("sync", slot[:, 768:896], vscr[pan, 2 * b, :, col:col + 128], qsem)
        p.dma("sync", slot[:, 896:1024], vscr[pan, 2 * b + 1, :, col:col + 128], qsem)
        c = p.dma("sync", slot[:, 1024:1792], mscr[:, h * 768:(h + 1) * 768], qsem)
        return slot, qsem, c, par

    def stage1(b, h):
        slot, qsem, lcnt, par = load_head(b, h)
        sbank = banks[4 + par]
        p.wait("tensor", qsem, lcnt)
        p.wait_key("tensor", ("BbankS", par))
        for jc in range(2):
            p.op("tensor", lambda e, o=sbank[:, jc * 256:(jc + 1) * 256],
                 l=slot[:, 256 + jc * 128:256 + (jc + 1) * 128], r=slot[:, 0:256]:
                 e.matmul(o, l, r, start=True, stop=True),
                 sem="PE" if jc == 1 else None)
        pe_s = p.cnt["PE"]
        sct = STG[:, par * 1024:par * 1024 + 512]
        sd = STG[:, par * 1024 + 512:par * 1024 + 1024]
        p.wait("scalar", "PE", pe_s)
        p.wait_key("scalar", ("Bsct", par))
        sc_e = p.op("scalar", lambda e, o=sct, i=sbank[:, 0:512]: e.copy(o, i), sem="SC")
        p.set_key(("BbankS", par), "SC", sc_e)
        p.wait("gpsimd", "SC", sc_e)
        p.wait_key("gpsimd", ("Bsd", par))
        gp_c = p.op("gpsimd", lambda e, o=sd, i=sct, m=slot[:, 1024:1536]:
                    e.tensor_tensor(o, i, m, op=ALU.mult), sem="GP")
        p.set_key(("Bsct", par), "GP", gp_c)
        qd = STG[:, 2048 + par * 256: 2048 + (par + 1) * 256]
        p.wait("vector", qsem, lcnt)
        p.wait_key("vector", ("Bqd", par))
        dv_q = p.op("vector", lambda e, o=qd, i=slot[:, 0:256],
                    m=slot[:, 1536:1792]:
                    e.tensor_tensor(o, i, m, op=ALU.mult), sem="DV")
        sbf = STG[:, 2560 + par * 128: 2560 + (par + 1) * 128]
        kk = p.keys.get(("Schain", h))
        if kk is not None:
            p.wait("scalar", kk[0], kk[1])
        p.wait_key("scalar", ("Bsbf", par))
        sc_b = p.op("scalar", lambda e, o=sbf, i=S[:, h * 128:(h + 1) * 128]:
                    e.copy(o, i), sem="SC")
        return dict(b=b, h=h, slot=slot, par=par, gp=gp_c, dvq=dv_q, scb=sc_b,
                    qd=qd, sd=sd, sbf=sbf)

    def stage2(r):
        b, h, slot, par = r["b"], r["h"], r["slot"], r["par"]
        pbank = banks[6 + par]
        p.wait("tensor", "GP", r["gp"])
        p.wait("tensor", "DV", r["dvq"])
        p.wait("tensor", "SC", r["scb"])
        p.wait_key("tensor", ("BbankP", par))
        for ic in range(2):
            o = pbank[:, ic * 128:(ic + 1) * 128]
            p.op("tensor", lambda e, o=o, l=r["sd"][:, ic * 128:(ic + 1) * 128],
                 v=slot[:, 768:896]: e.matmul(o, l, v, start=True, stop=False))
            p.op("tensor", lambda e, o=o, l=r["sd"][:, 256 + ic * 128:256 + (ic + 1) * 128],
                 v=slot[:, 896:1024]: e.matmul(o, l, v, start=False, stop=False))
            p.op("tensor", lambda e, o=o, l=r["qd"][:, ic * 128:(ic + 1) * 128],
                 v=r["sbf"]: e.matmul(o, l, v, start=False, stop=True),
                 sem="PE" if ic == 1 else None)
        pe_P = p.cnt["PE"]
        p.set_key(("Bsd", par), "PE", pe_P)
        p.set_key(("Bqd", par), "PE", pe_P)
        p.set_key(("Bsbf", par), "PE", pe_P)
        cbank = banks[2 + par]
        p.wait_key("tensor", ("BbankC", par))
        p.op("tensor", lambda e, o=cbank[:, 0:128], l=slot[:, 512:640], v=slot[:, 768:896]:
             e.matmul(o, l, v, start=True, stop=False))
        pe_C = p.op("tensor", lambda e, o=cbank[:, 0:128], l=slot[:, 640:768],
                    v=slot[:, 896:1024]:
                    e.matmul(o, l, v, start=False, stop=True), sem="PE")
        p.wait("scalar", "PE", pe_P)
        p.wait_key("scalar", "BattnSC")
        p.wait_key("scalar", "BattnDV")
        src = pbank[:, 0:256].rearrange("q (ic f) -> q ic f", ic=2)
        dst = attn.rearrange("q (ic w) -> q ic w", ic=2)[:, :, h * 128:(h + 1) * 128]
        sc = p.op("scalar", lambda e, o=dst, i=src: e.copy(o, i), sem="SC")
        p.set_key(("BbankP", par), "SC", sc)
        p.set_key(("Battnw", b), "SC", sc)
        p.wait("vector", "PE", pe_C)
        kk = p.keys.get(("Schain", h))
        p.wait("vector", kk[0], kk[1])
        p.wait("vector", "SC", r["scb"])
        dvs = p.op("vector", lambda e, o=S[:, h * 128:(h + 1) * 128],
                   s=bdv[:, h:h + 1], c=cbank[:, 0:128]:
                   e.scalar_tensor_tensor(o, o, s, c, op0=ALU.mult, op1=ALU.add),
                   sem="DV")
        p.set_key(("Schain", h), "DV", dvs)
        p.set_key(("BbankC", par), "DV", dvs)
        p.set_key(("Bslot", par), "DV", dvs)

    prev = None
    for b in range(NB):
        gsem = "GL0" if b % 2 == 0 else "GL1"
        gslot = gate_slot[b % 2]
        p.wait_key("sync", ("Bgate", b % 2))
        if b < 2:
            kk = p.keys.get("Baggdone")
            p.wait("sync", kk[0], kk[1])
        p.dma("sync", gslot[:, 0:4096], gscr[2 * b, :, :], gsem)
        g_c = p.dma("sync", gslot[:, 4096:8192], gscr[2 * b + 1, :, :], gsem)
        p.set_key(("Bgater", b % 2), gsem, g_c)

        for h in range(H):
            rec = stage1(b, h)
            if prev is not None:
                stage2(prev)
            prev = rec
        stage2(prev)
        prev = None

        p.wait_key("vector", ("Bgater", b % 2))
        lastw = p.keys.get(("Battnw", b))
        for ic in range(2):
            att = attn[:, ic * 4096:(ic + 1) * 4096]
            zb = zbf[:, ic * 4096:(ic + 1) * 4096]
            p.wait("scalar", lastw[0], lastw[1])
            p.wait_key("scalar", ("Bzb", ic))
            ssq = lnt[:, 32 + ic:33 + ic]
            sc1 = p.op("scalar", lambda e, o=zb, i=att, a=ssq:
                       e.activation(o, i, AF.Square, accum_out=a), sem="SC")
            p.wait("scalar", "SC", sc1)
            lnv = lnt[:, 34 + ic:35 + ic]
            sc2 = p.op("scalar", lambda e, o=lnv, i=ssq:
                       e.activation(o, i, AF.Ln, scale=1.0 / D, bias=t["epsc"][:, 0:1]), sem="SC")
            p.wait("scalar", "SC", sc2)
            rstd = lnt[:, 36 + ic:37 + ic]
            sc3 = p.op("scalar", lambda e, o=rstd, i=lnv:
                       e.activation(o, i, AF.Exp, scale=-0.5), sem="SC")
            gat = gate_slot[b % 2][:, ic * 4096:(ic + 1) * 4096]
            p.wait("vector", "SC", sc3)
            dv1 = p.op("vector", lambda e, o=zb, i=att, s=rstd, g=gat:
                       e.scalar_tensor_tensor(o, i, s, g, op0=ALU.mult, op1=ALU.mult),
                       sem="DV")
            if ic == 1:
                p.set_key(("Bgate", b % 2), "DV", dv1)
                p.set_key("BattnDV", "DV", dv1)
            tc = 2 * b + ic
            for grp in range(8):
                tbk = banks[grp % 2].ap().bitcast(BF16)
                p.wait("tensor", "DV", dv1)
                p.wait_key("tensor", ("Bztb", grp % 2))
                for j in range(4):
                    c = grp * 4 + j
                    p.op("tensor", lambda e, o=tbk[:, j * 128:(j + 1) * 128],
                         i=zb[:, c * 128:(c + 1) * 128], idt=ident[:, :]:
                         e.transpose(o, i, idt),
                         sem="PE" if j == 3 else None)
                pe_t = p.cnt["PE"]
                stg = STG[:, 2816 + (grp % 2) * 512: 2816 + (grp % 2 + 1) * 512]
                p.wait("vector", "PE", pe_t)
                p.wait_key("vector", ("Bzstg", grp % 2))
                dv_e = p.op("vector", lambda e, o=stg, i=tbk[:, 0:512]:
                            e.tensor_copy(o, i), sem="DV")
                p.set_key(("Bztb", grp % 2), "DV", dv_e)
                p.wait("sync", "DV", dv_e)
                st_c = p.dma("sync", zscr[:, tc * 4096 + grp * 512: tc * 4096 + (grp + 1) * 512],
                             stg, f"SS{grp % 2}")
                p.set_key(("Bzstg", grp % 2), f"SS{grp % 2}", st_c)
            p.set_key(("Bzb", ic), "PE", p.cnt["PE"])
        p.set_key("BattnSC", "SC", p.cnt["SC"])

    p.set_key("B_stores", "FG", p.barrier())
    p.set_key("B_PE_done", "PE", p.cnt["PE"])



def emit_C(p, t):
    ext, PX, STG, banks = t["ext"], t["PX"], t["STG"], t["banks"]
    load, free = p._slab_load, p._slab_free
    zscr, plscr = t["scr"]["z_scr"], t["scr"]["pl1_scr"]

    p.wait_key("gpsimd", "B_stores")
    p.wait_key("gpsimd", "B_PE_done")
    p.wait("gpsimd", "DV", p.cnt["DV"])
    p.wait("gpsimd", "SC", p.cnt["SC"])
    zl = None
    for i in range(8):
        w = NTC * D // 8
        zl = p.dma("gpsimd", PX[:, i * w:(i + 1) * w], zscr[:, i * w:(i + 1) * w], "LD")
    p.set_key("z_resident", "LD", zl)

    xres = STG[:, 0:2048].bitcast(F32)
    stg_o = STG[:, 2048:4096].bitcast(F32)
    for par in range(2):
        p.seed_key(("Cxs", par), "PE")
        p.seed_key(("Cstg", par), "PE")
        p.seed_key(("Css", par), f"SS{2 + par}")

    nwc = t["nwc"]
    for s in range(16):
        src = ext["wo"][:, s * NFI * 256:(s + 1) * NFI * 256]
        slot, lsem, lcnt = load(src, NFI * 256)
        p.wait("vector", lsem, lcnt)
        p.wait_key("vector", ("Cnw", 0))
        dvw = None
        for c in range(NFI):
            dvw = p.op("vector", lambda e, o=slot[:, c * 256:(c + 1) * 256],
                       i=slot[:, c * 256:(c + 1) * 256], sc=nwc[:, c:c + 1]:
                       e.tensor_scalar(o, i, sc, None, op0=ALU.mult),
                       sem="DV" if c == NFI - 1 else None)
        p.wait("tensor", "DV", dvw)
        if s == 0:
            p.wait_key("tensor", "z_resident")
        for tc in range(NTC):
            par = tc % 2
            xsem = "XL0" if par == 0 else "XL1"
            p.wait_key("scalar", ("Cxs", par))
            xc = p.dma("scalar", xres[:, par * 256:(par + 1) * 256],
                       ext["x_tm"][tc * 128:(tc + 1) * 128, s * 256:(s + 1) * 256], xsem)
            b = tc % 8
            p.wait_key("tensor", ("bank", b))
            for c in range(NFI):
                p.op("tensor", lambda e, o=banks[b][:, 0:256],
                     l=PX[:, tc * 4096 + c * 128: tc * 4096 + (c + 1) * 128],
                     r=slot[:, c * 256:(c + 1) * 256],
                     st=(c == 0), sp=(c == NFI - 1):
                     e.matmul(o, l, r, start=st, stop=sp),
                     sem="PE" if c == NFI - 1 else None)
            pe_c = p.cnt["PE"]
            p.wait("vector", "PE", pe_c)
            p.wait("vector", xsem, xc)
            p.wait_key("vector", ("Cstg", par))
            dv_c = p.op("vector", lambda e, o=stg_o[:, par * 256:(par + 1) * 256],
                        i=banks[b][:, 0:256], x=xres[:, par * 256:(par + 1) * 256]:
                        e.tensor_tensor(o, i, x, op=ALU.add), sem="DV")
            p.set_key(("bank", b), "DV", dv_c)
            p.set_key(("Cxs", par), "DV", dv_c)
            p.wait("sync", "DV", dv_c)
            p.wait_key("sync", ("Css", par))
            st_c = p.dma("sync", plscr[tc, :, s * 256:(s + 1) * 256],
                         stg_o[:, par * 256:(par + 1) * 256], f"SS{2 + par}")
            p.set_key(("Cstg", par), f"SS{2 + par}", st_c)
        free()
        p.set_key(("Cnw", 0), "PE", p.cnt["PE"])
    p.set_key("C_stores", "FG", p.barrier())
    p.set_key("C_PE_done", "PE", p.cnt["PE"])



def emit_LN1(p, t):
    PX, STG, banks, ident, lnt = t["PX"], t["STG"], t["banks"], t["ident"], t["lnt"]
    plscr, x1scr, x1fm = t["scr"]["pl1_scr"], t["scr"]["x1_scr"], t["scr"]["x1fm_scr"]
    row = [PX[:, 0:8192].bitcast(F32), PX[:, 8192:16384].bitcast(F32)]
    x1row = PX[:, 16384:24576].bitcast(F32)
    x1bf = PX[:, 24576:28672]

    cs = p.keys.get("C_stores")
    p.seed_key(("L1row", 0), "PE")
    p.seed_key(("L1row", 1), "PE")
    p.seed_key(("L1x1row",), "PE")
    p.seed_key(("L1x1",), "PE")
    for i in range(2):
        p.seed_key(("L1stg", i), f"SS{i}")

    for tc in range(NTC):
        par = tc % 2
        xsem = "XL0" if par == 0 else "XL1"
        p.wait_key("scalar", ("L1row", par))
        p.wait("scalar", cs[0], cs[1])
        rc_ = p.dma("scalar", row[par][:, :], plscr[tc, :, :], xsem)
        p.wait("vector", xsem, rc_)
        for i in range(8):
            p.op("vector", lambda e, o=lnt[:, 6 * i:6 * (i + 1)],
                 i2=row[par][:, i * 512:(i + 1) * 512]:
                 e.bn_stats(o, i2), sem="DV" if i == 7 else None)
        p.wait("vector", "DV", p.cnt["DV"])
        dva = p.op("vector", lambda e, o=lnt[:, 48:50], i=lnt[:, 0:48]:
                   e.bn_aggr(o, i), sem="DV")
        p.wait("scalar", "DV", dva)
        s1 = p.op("scalar", lambda e, o=lnt[:, 50:51], i=lnt[:, 49:50]:
                  e.activation(o, i, AF.Ln, bias=t["epsc"][:, 0:1]), sem="SC")
        p.wait("scalar", "SC", s1)
        s2 = p.op("scalar", lambda e, o=lnt[:, 51:52], i=lnt[:, 50:51]:
                  e.activation(o, i, AF.Exp, scale=-0.5), sem="SC")
        p.wait("vector", "SC", s2)
        p.wait_key("vector", ("L1x1row",))
        dvn = p.op("vector", lambda e, o=x1row[:, :], i=row[par][:, :],
                   m=lnt[:, 48:49], r=lnt[:, 51:52]:
                   e.tensor_scalar(o, i, m, r, op0=ALU.subtract, op1=ALU.mult), sem="DV")
        p.set_key(("L1row", par), "DV", dvn)
        p.wait("sync", "DV", dvn)
        p.wait_key("sync", ("L1xrs", tc % 2))
        st1 = p.dma("sync", x1scr[tc, :, :], x1row[:, :], f"SS{2 + tc % 2}")
        p.set_key(("L1x1row",), f"SS{2 + tc % 2}", st1)
        p.set_key(("L1xrs", tc % 2), f"SS{2 + tc % 2}", st1)
        p.wait("vector", "DV", dvn)
        p.wait_key("vector", ("L1x1",))
        dvb = p.op("vector", lambda e, o=x1bf[:, :], i=x1row[:, :]:
                   e.tensor_copy(o, i), sem="DV")
        for grp in range(8):
            tbk = banks[grp % 2].ap().bitcast(BF16)
            p.wait("tensor", "DV", dvb)
            p.wait_key("tensor", ("L1tb", grp % 2))
            if tc == 0 and grp < 2:
                p.wait_key("tensor", ("bank", grp % 2))
            for j in range(4):
                c = grp * 4 + j
                p.op("tensor", lambda e, o=tbk[:, j * 128:(j + 1) * 128],
                     i=x1bf[:, c * 128:(c + 1) * 128], idt=ident[:, :]:
                     e.transpose(o, i, idt),
                     sem="PE" if j == 3 else None)
            pe_t = p.cnt["PE"]
            stg = STG[:, (grp % 2) * 512:(grp % 2 + 1) * 512]
            p.wait("scalar", "PE", pe_t)
            p.wait_key("scalar", ("L1stg", grp % 2))
            sc_e = p.op("scalar", lambda e, o=stg, i=tbk[:, 0:512]: e.copy(o, i), sem="SC")
            p.set_key(("L1tb", grp % 2), "SC", sc_e)
            p.wait("sync", "SC", sc_e)
            st_c = p.dma("sync", x1fm[:, tc * 4096 + grp * 512: tc * 4096 + (grp + 1) * 512],
                         stg, f"SS{grp % 2}")
            p.set_key(("L1stg", grp % 2), f"SS{grp % 2}", st_c)
        p.set_key(("L1x1",), "PE", p.cnt["PE"])
        if tc % 4 == 3:
            p.set_key(("L1tc", tc), "FG", p.barrier())
    p.set_key("L1_stores", "FG", p.cnt["FG"])



def emit_D1(p, t):
    ext, PX, banks = t["ext"], t["PX"], t["banks"]
    x1fm = t["scr"]["x1fm_scr"]
    load, free = p._slab_load, p._slab_free

    xsl = [PX[:, 28672:45056], PX[:, 0:16384]]
    Pfm = PX[:, 49152:65536]
    p.seed_key("D1_pfm_gate", "PE")
    for par in range(2):
        p.seed_key(("D1x", par), "PE")

    for k in range(4):
        par = k % 2
        xsem = "GL0" if par == 0 else "GL1"
        p.wait_key("scalar", ("D1x", par))
        kk = p.keys.get(("L1tc", 4 * k + 3))
        p.wait("scalar", kk[0], kk[1])
        if k == 1:
            p.wait_key("scalar", "L1_stores")
        xc = p.dma("scalar", xsl[par][:, :], x1fm[:, k * 16384:(k + 1) * 16384], xsem)
        for rcn in range(8):
            src2 = ext["u1"][:, rcn * NFI * 128:(rcn + 1) * NFI * 128]
            slot, lsem, lcnt = load(src2, NFI * 128)
            b = rcn % 8
            p.wait_key("tensor", ("bank", b))
            p.wait("tensor", xsem, xc)
            p.wait("tensor", lsem, lcnt)
            for c in range(NFI):
                rhs = xsl[par].rearrange("q (tc cc j) -> q tc cc j", tc=4, cc=NFI)[:, :, c, :]
                p.op("tensor", lambda e, o=banks[b][:, :],
                     l=slot[:, c * 128:(c + 1) * 128], r=rhs,
                     st=(c == 0), sp=(c == NFI - 1):
                     e.matmul(o, l, r, start=st, stop=sp),
                     sem="PE" if c == NFI - 1 else None)
            pe_c = p.cnt["PE"]
            free()
            p.wait("vector", "PE", pe_c)
            if k == 0 and rcn == 0:
                p.wait_key("vector", "D1_pfm_gate")
            dv_c = p.op("vector", lambda e,
                        o=Pfm[:, rcn * T + k * 512: rcn * T + (k + 1) * 512],
                        i=banks[b][:, :]: e.tensor_copy(o, i), sem="DV")
            p.set_key(("bank", b), "DV", dv_c)
            if rcn == 7:
                p.set_key(("D1x", par), "DV", dv_c)
    p.set_key("Pfm_ready", "DV", p.cnt["DV"])



def emit_D23(p, t):
    ext, PX, STG, banks = t["ext"], t["PX"], t["STG"], t["banks"]
    b1c = t["b1c"]
    load, free = p._slab_load, p._slab_free
    Pfm = PX[:, 49152:65536]
    act = [PX[:, i * 2048:(i + 1) * 2048] for i in range(8)]
    Z = PX[:, 16384:49152].bitcast(F32)
    silug = STG[:, 0:2048]

    pw = p.keys.get("Pfm_ready")
    for tt in range(4):
        p.seed_key(("D2sg", tt), "ST")
    for i in range(8):
        p.seed_key(("D2act", i), "PE")
    p.seed_key("D3z_gate", "PE")

    first_group = True
    idx = 0

    def h_chunk(fcpos, is_gate, act_slot):
        nonlocal idx
        src = ext["v1"][:, fcpos * 8 * 128:(fcpos + 1) * 8 * 128]
        slot, lsem, lcnt = load(src, 8 * 128)
        p.wait("tensor", lsem, lcnt)
        if idx == 0:
            p.wait("tensor", pw[0], pw[1])
        idx += 1
        for tt in range(4):
            b = tt % 4
            p.wait_key("tensor", ("bank", b))
            for rcn in range(8):
                p.op("tensor", lambda e, o=banks[b][:, :],
                     l=slot[:, rcn * 128:(rcn + 1) * 128],
                     r=Pfm[:, rcn * T + tt * 512: rcn * T + (tt + 1) * 512],
                     st=(rcn == 0), sp=(rcn == 7):
                     e.matmul(o, l, r, start=st, stop=sp),
                     sem="PE" if rcn == 7 else None)
            pe_c = p.cnt["PE"]
            if is_gate:
                p.wait("scalar", "PE", pe_c)
                p.wait_key("scalar", ("D2sg", tt))
                sc_c = p.op("scalar", lambda e, o=silug[:, tt * 512:(tt + 1) * 512],
                            i=banks[b][:, :], bb=b1c[:, fcpos:fcpos + 1]:
                            e.activation(o, i, AF.Silu, bias=bb), sem="SC")
                p.set_key(("bank", b), "SC", sc_c)
                p.set_key(("D2sgw", tt), "SC", sc_c)
            else:
                p.wait("vector", "PE", pe_c)
                kk = p.keys.get(("D2sgw", tt))
                p.wait("vector", kk[0], kk[1])
                p.wait_key("vector", ("D2act", act_slot))
                dv_c = p.op("vector", lambda e,
                            o=act[act_slot][:, tt * 512:(tt + 1) * 512],
                            i=banks[b][:, :], bb=b1c[:, fcpos:fcpos + 1],
                            g=silug[:, tt * 512:(tt + 1) * 512]:
                            e.scalar_tensor_tensor(o, i, bb, g, op0=ALU.add, op1=ALU.mult),
                            sem="DV")
                p.set_key(("bank", b), "DV", dv_c)
                p.set_key(("D2sg", tt), "DV", dv_c)
                if tt == 3:
                    p.set_key(("actready", act_slot), "DV", dv_c)
        free()

    for g0 in range(0, GCH, 4):
        gsz = min(4, GCH - g0)
        for j in range(gsz):
            fc = g0 + j
            h_chunk(2 * fc, True, fc % 8)
            h_chunk(2 * fc + 1, False, fc % 8)
        src = ext["u2"][:, g0 * RANK:(g0 + gsz) * RANK]
        slot, lsem, lcnt = load(src, gsz * RANK)
        p.wait("tensor", lsem, lcnt)
        kk = p.keys.get(("actready", (g0 + gsz - 1) % 8))
        p.wait("tensor", kk[0], kk[1])
        if first_group:
            p.wait_key("vector", "D3z_gate")
        for rcn in range(8):
            for tt in range(4):
                b = 4 + (rcn * 4 + tt) % 4
                p.wait_key("tensor", ("bank", b))
                for j in range(gsz):
                    p.op("tensor", lambda e, o=banks[b][:, :],
                         l=slot[:, j * RANK + rcn * 128: j * RANK + (rcn + 1) * 128],
                         r=act[(g0 + j) % 8][:, tt * 512:(tt + 1) * 512],
                         st=(j == 0), sp=(j == gsz - 1):
                         e.matmul(o, l, r, start=st, stop=sp),
                         sem="PE" if j == gsz - 1 else None)
                pe_c = p.cnt["PE"]
                p.wait("vector", "PE", pe_c)
                zsl = Z[:, rcn * T + tt * 512: rcn * T + (tt + 1) * 512]
                if first_group:
                    dv_c = p.op("vector", lambda e, o=zsl, i=banks[b][:, :]:
                                e.tensor_copy(o, i), sem="DV")
                else:
                    p.wait_key("vector", ("Zsl", (rcn, tt)))
                    dv_c = p.op("vector", lambda e, o=zsl, i=banks[b][:, :], z=zsl:
                                e.tensor_tensor(o, i, z, op=ALU.add), sem="DV")
                p.set_key(("bank", b), "DV", dv_c)
                p.set_key(("Zsl", (rcn, tt)), "DV", dv_c)
        grp_pe = p.cnt["PE"]
        for j in range(gsz):
            p.set_key(("D2act", (g0 + j) % 8), "PE", grp_pe)
        first_group = False
        free()
    p.set_key("Z_ready", "DV", p.cnt["DV"])
    p.set_key("Pfm_dead", "PE", p.cnt["PE"])



def emit_D4(p, t):
    ext, PX, STG, banks = t["ext"], t["PX"], t["STG"], t["banks"]
    ones, lnt = t["ones"], t["lnt"]
    x1scr, pl2scr, OUT = t["scr"]["x1_scr"], t["scr"]["pl2_scr"], t["OUT"]
    load, free = p._slab_load, p._slab_free
    Z = PX[:, 16384:49152].bitcast(F32)
    Zbf = PX[:, 0:16384]
    x1res = STG[:, 0:2048].bitcast(F32)
    stg_o = STG[:, 2048:4096].bitcast(F32)

    zr = p.keys.get("Z_ready")
    pfd = p.keys.get("Pfm_dead")
    p.wait("vector", zr[0], zr[1])
    p.wait("vector", pfd[0], pfd[1])
    dvz = None
    for rcn in range(8):
        dvz = p.op("vector", lambda e, o=Zbf[:, rcn * T:(rcn + 1) * T],
                   i=Z[:, rcn * T:(rcn + 1) * T]: e.tensor_copy(o, i), sem="DV")
    p.set_key("Zbf_ready", "DV", dvz)

    for par in range(2):
        p.seed_key(("D4x", par), "PE")
        p.seed_key(("D4stg", par), f"SS{par}")

    lw = p.keys.get("L1_stores")
    b2t = t["b2"]
    for s in range(8):
        src2 = ext["v2"][:, s * 4096:(s + 1) * 4096]
        slot, lsem, lcnt = load(src2, 4096)
        bpar = s % 2
        bsem = "B20" if bpar == 0 else "B21"
        p.wait_key("gpsimd", ("D4b2", bpar))
        b2c = p.dma("gpsimd", b2t[0:1, bpar * 512:(bpar + 1) * 512],
                    ext["b2"][0:1, s * 512:(s + 1) * 512], bsem)
        p.wait("tensor", lsem, lcnt)
        p.wait("tensor", bsem, b2c)
        if s == 0:
            p.wait_key("tensor", "Zbf_ready")
        for tc in range(NTC):
            par = tc % 2
            xsem = "XL0" if par == 0 else "XL1"
            p.wait_key("scalar", ("D4x", par))
            if s == 0 and tc < 2:
                p.wait("scalar", lw[0], lw[1])
            xc = p.dma("scalar", x1res[:, par * 512:(par + 1) * 512],
                       x1scr[tc, :, s * 512:(s + 1) * 512], xsem)
            b = tc % 8
            p.wait_key("tensor", ("bank", b))
            for rcn in range(8):
                p.op("tensor", lambda e, o=banks[b][:, :],
                     l=Zbf[:, rcn * T + tc * 128: rcn * T + tc * 128 + 128],
                     r=slot[:, rcn * 512:(rcn + 1) * 512],
                     st=(rcn == 0), sp=False:
                     e.matmul(o, l, r, start=st, stop=sp))
            p.op("tensor", lambda e, o=banks[b][:, :],
                 l=ones[0:1, 0:128], r=b2t[0:1, bpar * 512:(bpar + 1) * 512]:
                 e.matmul(o, l, r, start=False, stop=True), sem="PE")
            pe_c = p.cnt["PE"]
            p.wait("vector", "PE", pe_c)
            p.wait("vector", xsem, xc)
            p.wait_key("vector", ("D4stg", par))
            dv_c = p.op("vector", lambda e, o=stg_o[:, par * 512:(par + 1) * 512],
                        i=banks[b][:, :], x=x1res[:, par * 512:(par + 1) * 512]:
                        e.tensor_tensor(o, i, x, op=ALU.add), sem="DV")
            p.set_key(("bank", b), "DV", dv_c)
            p.set_key(("D4x", par), "DV", dv_c)
            p.wait("sync", "DV", dv_c)
            st_c = p.dma("sync", pl2scr[tc, :, s * 512:(s + 1) * 512],
                         stg_o[:, par * 512:(par + 1) * 512], f"SS{par}")
            p.set_key(("D4stg", par), f"SS{par}", st_c)
        free()
        p.set_key(("D4b2", bpar), "PE", p.cnt["PE"])
    p.set_key("D4a_stores", "FG", p.barrier())

    row = [PX[:, 16384:24576].bitcast(F32), PX[:, 24576:32768].bitcast(F32)]
    d4s = p.keys.get("D4a_stores")
    p.seed_key(("L2row", 0), "DV")
    p.seed_key(("L2row", 1), "DV")
    for tc in range(NTC):
        par = tc % 2
        xsem = "XL0" if par == 0 else "XL1"
        p.wait_key("scalar", ("L2row", par))
        p.wait("scalar", d4s[0], d4s[1])
        rc_ = p.dma("scalar", row[par][:, :], pl2scr[tc, :, :], xsem)
        p.wait("vector", xsem, rc_)
        for i in range(8):
            p.op("vector", lambda e, o=lnt[:, 6 * i:6 * (i + 1)],
                 i2=row[par][:, i * 512:(i + 1) * 512]:
                 e.bn_stats(o, i2), sem="DV" if i == 7 else None)
        p.wait("vector", "DV", p.cnt["DV"])
        dva = p.op("vector", lambda e, o=lnt[:, 48:50], i=lnt[:, 0:48]:
                   e.bn_aggr(o, i), sem="DV")
        p.wait("scalar", "DV", dva)
        s1 = p.op("scalar", lambda e, o=lnt[:, 52:53], i=lnt[:, 49:50]:
                  e.activation(o, i, AF.Ln, bias=t["epsc"][:, 0:1]), sem="SC")
        p.wait("scalar", "SC", s1)
        s2 = p.op("scalar", lambda e, o=lnt[:, 53:54], i=lnt[:, 52:53]:
                  e.activation(o, i, AF.Exp, scale=-0.5), sem="SC")
        p.wait("vector", "SC", s2)
        dvn = p.op("vector", lambda e, o=row[par][:, :], i=row[par][:, :],
                   m=lnt[:, 48:49], r=lnt[:, 53:54]:
                   e.tensor_scalar(o, i, m, r, op0=ALU.subtract, op1=ALU.mult), sem="DV")
        p.wait("sync", "DV", dvn)
        st_c = p.dma("sync", OUT[tc * 128:(tc + 1) * 128, :], row[par][:, :], f"SS{2 + par}")
        p.set_key(("L2row", par), f"SS{2 + par}", st_c)


def emit_taps(p, t):
    if not t["tapped"]:
        return
    for s in ["ST", "SS0", "SS1", "SS2", "SS3"]:
        p.wait("sync", s, p.cnt[s])
    p.wait("sync", "DV", p.cnt["DV"])
    p.wait("sync", "SC", p.cnt["SC"])
    p.wait("sync", "PE", p.cnt["PE"])
    p.wait("sync", "GP", p.cnt["GP"])
    for name, dst in t["tapped"].items():
        if name == "S_dump":
            continue
        p.dma("sync", dst, t["scr"][name], "ST")


def emit_final(p, t):
    for s in ["ST", "SS0", "SS1", "SS2", "SS3"]:
        p.wait("sync", s, p.cnt[s])
    p.wait("sync", "PE", p.cnt["PE"])
    p.wait("sync", "DV", p.cnt["DV"])
    p.wait("sync", "SC", p.cnt["SC"])
    p.wait("sync", "GP", p.cnt["GP"])
    p.wait("sync", "CC", p.cnt["CC"])
    p.wait("sync", "LD", p.cnt["LD"])
    p.wait("sync", "WL0", p.cnt["WL0"])
    p.wait("sync", "WL1", p.cnt["WL1"])
    p.wait("sync", "QL0", p.cnt["QL0"])
    p.wait("sync", "QL1", p.cnt["QL1"])
    p.wait("sync", "GL0", p.cnt["GL0"])
    p.wait("sync", "GL1", p.cnt["GL1"])
    p.wait("sync", "XL0", p.cnt["XL0"])
    p.wait("sync", "XL1", p.cnt["XL1"])



def kernel(**inputs):
    in_maps = pack_inputs(inputs)
    nc = build_program(taps=tuple(TAPS), stop_after=STOP_AFTER)
    res = run_bass_kernel_spmd(nc, in_maps, core_ids=list(range(NCORES)))
    out = np.concatenate([res.results[c]["out"] for c in range(NCORES)], axis=0)
    return out.reshape(BATCH, SEQ, D)


# version 09
# speedup vs baseline: 1.1797x; 1.1797x over previous
import numpy as np
from contextlib import ExitStack

import concourse.bass as bass
import concourse.mybir as mybir
from concourse.bass_utils import run_bass_kernel_spmd

F32 = mybir.dt.float32
BF16 = mybir.dt.bfloat16
I32 = mybir.dt.int32
AF = mybir.ActivationFunctionType
ALU = mybir.AluOpType

BATCH, SEQ, D = 2, 8192, 4096
H, HD, BS = 32, 128, 256
LAYER_IDX, N_LAYERS = 1, 32
EPS = 1e-5
INTER, RANK = 11008, 1024

NCORES = 8
T = 2048
NB = T // BS
NFI = D // 128
NTC = T // 128
GCH = INTER // 128
FCH = 2 * GCH

TAPS = []
STOP_AFTER = None


def _decays_np():
    base = 1.0 / 2.0 ** (8.0 / H)
    exponent = np.arange(H, dtype=np.float64) + 1.0
    factor = 1.0 - LAYER_IDX / (N_LAYERS - 1 + 1e-5) + 1e-5
    slope = (base ** exponent) * factor
    r = np.arange(BS, dtype=np.float64) + 1.0
    q_decay = np.exp(-slope[:, None] * r[None, :])
    k_decay = np.exp(-slope[:, None] * (BS - r)[None, :])
    diff = r[:, None] - r[None, :]
    diag = slope[:, None, None] * diff[None, :, :]
    diag_decay = np.where(diag >= 0, np.exp(-diag), 0.0)
    block_decay = np.exp(-slope * BS)
    return slope, q_decay, k_decay, diag_decay, block_decay


def _img(a2d):
    rows, cols = a2d.shape
    c = rows // 128
    return np.ascontiguousarray(
        a2d.reshape(c, 128, cols).transpose(1, 0, 2).reshape(128, c * cols))


def pack_inputs(inp):
    x = np.asarray(inp["L0_attn_hidden_states"], np.float32)
    qkv_w = np.asarray(inp["L0_attn_qkv_w"], np.float32)
    out_w = np.asarray(inp["L0_attn_out_w"], np.float32)
    gate_w = np.asarray(inp["L0_attn_gate_w"], np.float32)
    norm_w = np.asarray(inp["L0_attn_norm_w"], np.float32)
    U1 = np.asarray(inp["L0_ffn_U1"], np.float32)
    V1 = np.asarray(inp["L0_ffn_V1"], np.float32)
    U2 = np.asarray(inp["L0_ffn_U2"], np.float32)
    V2 = np.asarray(inp["L0_ffn_V2"], np.float32)
    b1 = np.asarray(inp["L0_ffn_b1"], np.float32)
    b2 = np.asarray(inp["L0_ffn_b2"], np.float32)

    qkvT = np.ascontiguousarray(qkv_w.T)
    gateT = np.ascontiguousarray(gate_w.T)
    outT = np.ascontiguousarray(out_w.T)

    qk_cols = []
    for h in range(H):
        qk_cols.append(qkvT[:, 384 * h:384 * h + 128])
        qk_cols.append(qkvT[:, 384 * h + 128:384 * h + 256])
    wqk = np.concatenate([_img(c) for c in qk_cols], axis=1)

    v_cols = []
    for pan in range(16):
        h0, h1 = 2 * pan, 2 * pan + 1
        v_cols.append(np.concatenate(
            [qkvT[:, 384 * h0 + 256:384 * h0 + 384],
             qkvT[:, 384 * h1 + 256:384 * h1 + 384]], axis=1))
    wv = np.concatenate([_img(c) for c in v_cols], axis=1)

    wg = np.concatenate([_img(gateT[:, 256 * s:256 * (s + 1)]) for s in range(16)], axis=1)
    wo = np.concatenate([_img(outT[:, 256 * s:256 * (s + 1)]) for s in range(16)], axis=1)

    u1 = np.concatenate([_img(U1[:, 128 * rc:128 * (rc + 1)]) for rc in range(8)], axis=1)

    v1_order = []
    for g in range(GCH):
        v1_order += [g, GCH + g]
    v1 = np.concatenate([_img(V1[:, 128 * fc:128 * (fc + 1)]) for fc in v1_order], axis=1)

    u2 = np.ascontiguousarray(
        U2.reshape(GCH, 128, RANK).transpose(1, 0, 2).reshape(128, GCH * RANK))

    v2 = np.concatenate([_img(V2[:, 512 * s:512 * (s + 1)]) for s in range(8)], axis=1)

    slope, q_decay, k_decay, diag_decay, block_decay = _decays_np()
    qdm = np.zeros((128, H * 768), np.float32)
    for h in range(H):
        for jc in range(2):
            qdm[:, h * 768 + jc * 256: h * 768 + (jc + 1) * 256] = \
                diag_decay[h, :, jc * 128:(jc + 1) * 128].T
        qdm[:, h * 768 + 512: h * 768 + 768] = q_decay[h][None, :]
    kdec = np.zeros((128, H * 2), np.float32)
    for h in range(H):
        for jc in range(2):
            kdec[:, h * 2 + jc] = k_decay[h, jc * 128:(jc + 1) * 128]
    bdv = np.tile(block_decay.reshape(1, H), (128, 1)).astype(np.float32)

    ccw_all = []
    for core in range(NCORES):
        g = core % 4
        w = np.zeros((H, 4), np.float64)
        for r in range(g):
            w[:, r] = block_decay ** (8.0 * (g - 1 - r))
        ccw_all.append(np.tile(w.reshape(1, H * 4), (128, 1)).astype(np.float32))

    nwc = np.ascontiguousarray(norm_w.reshape(NFI, 128).T)
    b1c = np.stack([b1[128 * fc:128 * (fc + 1)] for fc in v1_order], axis=1)
    b2_pack = b2.reshape(1, D)

    shared = dict(wqk=wqk, wv=wv, wg=wg, wo=wo, u1=u1, v1=v1, u2=u2, v2=v2,
                  qdm=qdm, kdec=kdec, bdv=bdv, nwc=nwc,
                  b1c=np.ascontiguousarray(b1c),
                  b2=np.ascontiguousarray(b2_pack))

    xr = x.reshape(BATCH * SEQ, D)
    in_maps = []
    for core in range(NCORES):
        sl = xr[core * T:(core + 1) * T]
        m = dict(shared)
        m["x_tm"] = np.ascontiguousarray(sl)
        m["x_fm"] = _img(np.ascontiguousarray(sl.T))
        m["ccw"] = ccw_all[core]
        in_maps.append(m)
    return in_maps



ENGINES = ("sync", "scalar", "vector", "tensor", "gpsimd")


class Prog:
    def __init__(self, nc, es):
        self.nc, self.es = nc, es
        self.ops = {e: [] for e in ENGINES}
        self.sem, self.cnt = {}, {}
        self.keys = {}

    def new_sem(self, name):
        self.sem[name] = self.es.enter_context(self.nc.semaphore(name))
        self.cnt[name] = 0

    def emit(self, eng, fn):
        self.ops[eng].append(fn)

    def wait(self, eng, name, val):
        if val is None or val <= 0:
            return
        h = self.sem[name]
        self.emit(eng, lambda e, h=h, v=val: e.wait_ge(h, v))

    def wait_key(self, eng, key):
        g = self.keys.get(key)
        if g is not None:
            self.wait(eng, g[0], g[1])

    def set_key(self, key, sem, cnt):
        self.keys[key] = (sem, cnt)

    def seed_key(self, key, sem):
        self.keys[key] = (sem, self.cnt[sem])

    def op(self, eng, build, sem=None, amt=1):
        if sem:
            c = self.cnt[sem] = self.cnt[sem] + amt
            h = self.sem[sem]
            self.emit(eng, lambda e, b=build, h=h, a=amt: b(e).then_inc(h, a))
            return c
        self.emit(eng, lambda e, b=build: b(e))
        return None

    def barrier(self):
        for s in ["ST", "SS0", "SS1", "SS2", "SS3"]:
            self.wait("sync", s, self.cnt[s])
        c = self.cnt["FG"] = self.cnt["FG"] + 1
        h = self.sem["FG"]
        self.emit("sync", lambda e, h=h: e.sem_inc(h, 1))
        return c

    def dma(self, eng, out, in_, sem, **kw):
        c = self.cnt[sem] = self.cnt[sem] + 16
        h = self.sem[sem]
        self.emit(eng, lambda e, o=out, i=in_, h=h, kw=kw:
                  e.dma_start(out=o, in_=i, **kw).then_inc(h, 16))
        return c


def build_program(taps=(), stop_after=None):
    nc = bass.Bass("TRN2", target_bir_lowering=False, debug=False)

    ext = {}
    for name, shape in [
        ("x_tm", (T, D)), ("x_fm", (128, NFI * T)),
        ("wqk", (128, 64 * NFI * 128)), ("wv", (128, 16 * NFI * 256)),
        ("wg", (128, 16 * NFI * 256)), ("wo", (128, 16 * NFI * 256)),
        ("u1", (128, 8 * NFI * 128)), ("v1", (128, FCH * 8 * 128)),
        ("u2", (128, GCH * RANK)), ("v2", (128, 8 * 8 * 512)),
        ("qdm", (128, H * 768)),
        ("kdec", (128, H * 2)), ("bdv", (128, H)), ("ccw", (128, H * 4)),
        ("nwc", (128, NFI)), ("b1c", (128, FCH)), ("b2", (1, D)),
    ]:
        ext[name] = nc.dram_tensor(name, list(shape), F32, kind="ExternalInput").ap()
    OUT = nc.dram_tensor("out", [T, D], F32, kind="ExternalOutput").ap()

    scr = {}
    scr["q_scr"] = nc.dram_tensor("q_scr", [NB, H, 128, 256], BF16).ap()
    scr["k_scr"] = nc.dram_tensor("k_scr", [NB, H, 128, 256], BF16).ap()
    scr["v_scr"] = nc.dram_tensor("v_scr", [NTC, 16, 128, 256], BF16).ap()
    scr["kd_scr"] = nc.dram_tensor("kd_scr", [NB, H, 128, 256], BF16).ap()
    scr["g_scr"] = nc.dram_tensor("g_scr", [NTC, 128, D], BF16).ap()
    scr["z_scr"] = nc.dram_tensor("z_scr", [128, NTC * D], BF16).ap()
    scr["pl1_scr"] = nc.dram_tensor("pl1_scr", [NTC, 128, D], F32).ap()
    scr["x1_scr"] = nc.dram_tensor("x1_scr", [NTC, 128, D], F32).ap()
    scr["x1fm_scr"] = nc.dram_tensor("x1fm_scr", [128, NTC * D], BF16).ap()
    scr["qdm_scr"] = nc.dram_tensor("qdm_scr", [128, H * 768], BF16).ap()
    scr["pl2_scr"] = nc.dram_tensor("pl2_scr", [NTC, 128, D], F32).ap()
    cc_in_t = nc.dram_tensor("cc_in", [H * 128, 128], F32)
    cc_out_t = nc.dram_tensor("cc_out", [4 * H * 128, 128], F32)

    tapped = {}
    for name in taps:
        if name in ("S_dump", "SB_dump"):
            tapped[name] = nc.dram_tensor("tap_" + name, [128, H * 128], F32,
                                          kind="ExternalOutput").ap()
            continue
        s = scr[name]
        tapped[name] = nc.dram_tensor("tap_" + name, list(s.shape), s.dtype,
                                      kind="ExternalOutput").ap()

    es = ExitStack()
    with es:
        t = {"ext": ext, "OUT": OUT, "scr": scr, "cc_in": cc_in_t,
             "cc_out": cc_out_t, "tapped": tapped}
        sb = lambda n, shape, dt: es.enter_context(nc.sbuf_tensor(n, list(shape), dt))
        t["PX"] = sb("PX", (128, 65536), BF16)
        t["WS"] = [sb("WS0", (128, 8192), BF16), sb("WS1", (128, 8192), BF16)]
        t["STG"] = sb("STG", (128, 4096), BF16)
        t["CST"] = sb("CST", (128, 3584), BF16)
        t["kdec"] = sb("kdecs", (128, H * 2), F32)
        t["bdv"] = sb("bdvs", (128, H), F32)
        t["ccw"] = sb("ccws", (128, H * 4), F32)
        t["nwc"] = sb("nwcs", (128, NFI), F32)
        t["b1c"] = sb("b1cs", (128, FCH), F32)
        t["b2"] = sb("b2s", (1, 1024), BF16)
        t["ones"] = sb("oness", (1, 512), BF16)
        t["epsc"] = sb("epscs", (128, 1), F32)
        t["ident"] = sb("idents", (128, 128), BF16)
        t["ident_i"] = sb("identis", (128, 128), I32)
        t["S"] = sb("Ss", (128, H * 128), F32)
        t["lnt"] = sb("lnts", (128, 64), F32)
        t["banks"] = [es.enter_context(nc.psum_tensor(f"bk{i}", [128, 512], F32))
                      for i in range(8)]

        p = Prog(nc, es)
        for s in ["LD", "WL0", "WL1", "PE", "DV", "SC", "GP", "ST",
                  "SS0", "SS1", "SS2", "SS3", "FG", "B20", "B21",
                  "QL0", "QL1", "GL0", "GL1", "XL0", "XL1", "CC"]:
            p.new_sem(s)

        phases = [emit_consts, emit_A1, emit_A3, emit_A2, emit_sstart,
                  emit_B, emit_C, emit_LN1, emit_D1, emit_D23, emit_D4]
        for ph in phases:
            ph(p, t)
            if stop_after is not None and ph.__name__ == stop_after:
                break
        emit_taps(p, t)
        emit_final(p, t)

        block = es.enter_context(nc.Block())

        @block.sync
        def _(e):
            for f in p.ops["sync"]:
                f(e)

        @block.scalar
        def _(e):
            for f in p.ops["scalar"]:
                f(e)

        @block.vector
        def _(e):
            for f in p.ops["vector"]:
                f(e)

        @block.tensor
        def _(e):
            for f in p.ops["tensor"]:
                f(e)

        @block.gpsimd
        def _(e):
            for f in p.ops["gpsimd"]:
                f(e)

    return nc



def emit_consts(p, t):
    ext = t["ext"]
    for name, dst in [("kdec", t["kdec"]), ("bdv", t["bdv"]), ("ccw", t["ccw"]),
                      ("nwc", t["nwc"]), ("b1c", t["b1c"])]:
        p.dma("gpsimd", dst[:, :], ext[name], "LD", max_dma_last_dim=8192)
    p.dma("gpsimd", t["scr"]["qdm_scr"], ext["qdm"], "LD", max_dma_last_dim=8192)
    PX = t["PX"]
    for i in range(4):
        w = NFI * T // 4
        p.dma("gpsimd", PX[:, i * w:(i + 1) * w], ext["x_fm"][:, i * w:(i + 1) * w],
              "LD", max_dma_last_dim=8192)
    p.op("gpsimd", lambda e: e.iota(t["ident_i"][:, :], pattern=[[1, 128]],
                                    base=0, channel_multiplier=-1), sem="GP")
    p.op("gpsimd", lambda e: e.memset(t["ones"][:, :], 1.0), sem="GP")
    p.op("gpsimd", lambda e: e.memset(t["epsc"][:, :], EPS), sem="GP")
    p.op("gpsimd", lambda e: e.memset(t["S"][:, :], 0.0), sem="GP")
    gp = p.cnt["GP"]
    p.wait("vector", "GP", gp)
    c = p.op("vector", lambda e: e.tensor_scalar(t["ident"][:, :], t["ident_i"][:, :],
                                                 0.0, None, op0=ALU.is_equal), sem="DV")
    p.set_key("ident", "DV", c)
    p.set_key("consts", "LD", p.cnt["LD"])
    p.set_key("zeroS", "GP", gp)


def _slab_loader(p, t):
    WS = t["WS"]
    state = {"idx": 0, "free_pe": []}

    def load(src_ap, ncols):
        i = state["idx"]
        slot = i % 2
        sem = "WL0" if slot == 0 else "WL1"
        if i >= 2:
            p.wait("gpsimd", "PE", state["free_pe"][i - 2])
        c = p.dma("gpsimd", WS[slot][:, 0:ncols], src_ap, sem, max_dma_last_dim=8192)
        state["idx"] = i + 1
        return WS[slot], sem, c

    def free():
        state["free_pe"].append(p.cnt["PE"])

    return load, free



def emit_A1(p, t):
    ext, PX, STG, banks = t["ext"], t["PX"], t["STG"], t["banks"]
    load, free = _slab_loader(p, t)
    p._slab_load, p._slab_free = load, free

    for s in range(64):
        src = ext["wqk"][:, s * NFI * 128:(s + 1) * NFI * 128]
        slot, lsem, lcnt = load(src, NFI * 128)
        p.wait("tensor", lsem, lcnt)
        if s == 0:
            p.wait_key("tensor", "consts")
        h, isq = s // 2, (s % 2 == 0)
        for tt in range(4):
            b = (s * 4 + tt) % 8
            p.wait_key("tensor", ("bank", b))
            for c in range(NFI):
                p.op("tensor", lambda e, o=banks[b][:, :],
                     l=slot[:, c * 128:(c + 1) * 128],
                     r=PX[:, c * T + tt * 512: c * T + tt * 512 + 512],
                     st=(c == 0), sp=(c == NFI - 1):
                     e.matmul(o, l, r, start=st, stop=sp),
                     sem="PE" if c == NFI - 1 else None)
            pe_c = p.cnt["PE"]
            stg = STG[:, (tt % 4) * 512:(tt % 4) * 512 + 512]
            p.wait("scalar", "PE", pe_c)
            p.wait_key("scalar", ("A1stg", tt % 4))
            sc_c = p.op("scalar", lambda e, o=stg, i=banks[b][:, :]:
                        e.activation(o, i, AF.Silu), sem="SC")
            p.set_key(("bank", b), "SC", sc_c)
            p.wait("sync", "SC", sc_c)
            scrq = t["scr"]["q_scr"] if isq else t["scr"]["k_scr"]
            dst = scrq[2 * tt:2 * tt + 2, h].rearrange("b q t -> q b t")
            st_c = p.dma("sync", dst, stg.rearrange("q (b t) -> q b t", b=2), f"SS{tt % 4}")
            p.set_key(("A1stg", tt % 4), f"SS{tt % 4}", st_c)
        free()

    for pan in range(16):
        src = ext["wv"][:, pan * NFI * 256:(pan + 1) * NFI * 256]
        slot, lsem, lcnt = load(src, NFI * 256)
        p.wait("tensor", lsem, lcnt)
        for tc in range(NTC):
            b = tc % 8
            p.wait_key("tensor", ("bank", b))
            for c in range(NFI):
                p.op("tensor", lambda e, o=banks[b][:, 0:256],
                     l=PX[:, c * T + tc * 128: c * T + tc * 128 + 128],
                     r=slot[:, c * 256:(c + 1) * 256],
                     st=(c == 0), sp=(c == NFI - 1):
                     e.matmul(o, l, r, start=st, stop=sp),
                     sem="PE" if c == NFI - 1 else None)
            pe_c = p.cnt["PE"]
            stg = STG[:, 2048 + (tc % 4) * 256: 2048 + (tc % 4) * 256 + 256]
            p.wait("scalar", "PE", pe_c)
            p.wait_key("scalar", ("Avstg", tc % 4))
            sc_c = p.op("scalar", lambda e, o=stg, i=banks[b][:, 0:256]:
                        e.activation(o, i, AF.Silu), sem="SC")
            p.set_key(("bank", b), "SC", sc_c)
            p.wait("sync", "SC", sc_c)
            st_c = p.dma("sync", t["scr"]["v_scr"][tc, pan, :, :], stg, f"SS{tc % 4}")
            p.set_key(("Avstg", tc % 4), f"SS{tc % 4}", st_c)
        free()

    p.set_key("A1_stores", "FG", p.barrier())



def emit_A2(p, t):
    banks, CST, STG, S = t["banks"], t["CST"], t["STG"], t["S"]
    kdec, bdv, ident = t["kdec"], t["bdv"], t["ident"]
    kscr, vscr, kdscr = t["scr"]["k_scr"], t["scr"]["v_scr"], t["scr"]["kd_scr"]

    p.wait_key("sync", "A1_stores")
    p.wait_key("tensor", "ident")
    p.wait_key("vector", "zeroS")
    slots = [CST[:, 0:2048], STG[:, 0:2048]]
    for par in range(2):
        p.seed_key(("A2slot", par), "PE")
        p.seed_key(("A2kd", par), "SC")
        p.seed_key(("A2tb", par), "SC")
        p.seed_key(("A2cb", par), "SC")
        p.seed_key(("A2kds", par), f"SS{2 + par}")

    for b in range(NB):
        for g in range(H // 4):
            par = (b * 8 + g) % 2
            qsem = "QL0" if par == 0 else "QL1"
            slot = slots[par]
            p.wait_key("sync", ("A2slot", par))
            p.dma("sync", slot[:, 0:1024],
                  kscr[b, 4 * g:4 * g + 4].rearrange("h q t -> q h t"), qsem)
            p.dma("sync", slot[:, 1024:1536],
                  vscr[2 * b, 2 * g:2 * g + 2].rearrange("n q t -> q n t"), qsem)
            lcnt = p.dma("sync", slot[:, 1536:2048],
                         vscr[2 * b + 1, 2 * g:2 * g + 2].rearrange("n q t -> q n t"), qsem)
            tb = banks[4 + par].ap().bitcast(BF16)
            p.wait("tensor", qsem, lcnt)
            p.wait_key("tensor", ("A2tb", par))
            pe_tr = None
            for u in range(4):
                for jc in range(2):
                    pe_tr = p.op("tensor", lambda e,
                                 o=tb[:, (u * 2 + jc) * 128:(u * 2 + jc + 1) * 128],
                                 i=slot[:, u * 256 + jc * 128: u * 256 + (jc + 1) * 128],
                                 idt=ident[:, :]: e.transpose(o, i, idt),
                                 sem="PE" if (u == 3 and jc == 1) else None)
            kdx = t["PX"][:, 49152 + par * 1024: 49152 + (par + 1) * 1024]
            p.wait("vector", "PE", pe_tr)
            p.wait_key("vector", ("A2kdx", par))
            dv1 = None
            for u in range(4):
                for jc in range(2):
                    h = 4 * g + u
                    dv1 = p.op("vector", lambda e,
                               o=kdx[:, (u * 2 + jc) * 128:(u * 2 + jc + 1) * 128],
                               i=tb[:, (u * 2 + jc) * 128:(u * 2 + jc + 1) * 128],
                               s=kdec[:, 2 * h + jc:2 * h + jc + 1]:
                               e.tensor_scalar(o, i, s, None, op0=ALU.mult),
                               sem="DV" if (u == 3 and jc == 1) else None)
            p.set_key(("A2tb", par), "DV", dv1)
            p.wait("sync", "DV", dv1)
            p.wait_key("sync", ("A2kds", par))
            st_c = p.dma("sync", kdscr[b, 4 * g:4 * g + 4].rearrange("h q t -> q h t"),
                         kdx.rearrange("q (h t) -> q h t", h=4), f"SS{2 + par}")
            p.set_key(("A2kds", par), f"SS{2 + par}", st_c)
            cb = banks[2 + par]
            p.wait("tensor", "DV", dv1)
            p.wait_key("tensor", ("A2cb", par))
            pe_c = None
            for u in range(4):
                for jc in range(2):
                    pe_c = p.op("tensor", lambda e, o=cb[:, u * 128:(u + 1) * 128],
                                l=kdx[:, (u * 2 + jc) * 128:(u * 2 + jc + 1) * 128],
                                v=slot[:, 1024 + jc * 512 + u * 128: 1024 + jc * 512 + (u + 1) * 128],
                                st=(jc == 0), sp=(jc == 1):
                                e.matmul(o, l, v, start=st, stop=sp),
                                sem="PE" if (u == 3 and jc == 1) else None)
            p.set_key(("A2slot", par), "PE", pe_c)
            p.set_key(("A2kdx", par), "PE", pe_c)
            p.wait("vector", "PE", pe_c)
            dvs = None
            for u in range(4):
                h = 4 * g + u
                p.wait_key("vector", ("Schain", h))
                dvs = p.op("vector", lambda e, o=S[:, h * 128:(h + 1) * 128],
                           s=bdv[:, h:h + 1], c=cb[:, u * 128:(u + 1) * 128]:
                           e.scalar_tensor_tensor(o, o, s, c, op0=ALU.mult, op1=ALU.add),
                           sem="DV")
                p.set_key(("Schain", h), "DV", dvs)
            p.set_key(("A2cb", par), "DV", dvs)

    p.wait("sync", "DV", p.cnt["DV"])
    src = S[:, :].rearrange("d (h e) -> d h e", h=H)
    dst = t["cc_in"].ap().rearrange("(h d) e -> d h e", h=H)
    p.dma("sync", dst, src, "ST")
    ccflag = p.barrier()
    p.wait("gpsimd", "FG", ccflag)
    p.op("gpsimd", lambda e: e.collective_compute(
        "AllGather", ALU.bypass, replica_groups=[[0, 1, 2, 3], [4, 5, 6, 7]],
        ins=[t["cc_in"].ap().opt()], outs=[t["cc_out"].ap().opt()]), sem="CC")
    p.set_key("A2_done_PE", "PE", p.cnt["PE"])
    p.set_key("A2_stores", "FG", ccflag)



def emit_A3(p, t):
    ext, PX, STG, banks = t["ext"], t["PX"], t["STG"], t["banks"]
    load, free = p._slab_load, p._slab_free
    for par in range(2):
        p.seed_key(("A3stg", par), f"SS{par}")
    for pan in range(16):
        src = ext["wg"][:, pan * NFI * 256:(pan + 1) * NFI * 256]
        slot, lsem, lcnt = load(src, NFI * 256)
        p.wait("tensor", lsem, lcnt)
        for tc in range(NTC):
            b = 4 + tc % 4
            p.wait_key("tensor", ("bank", b))
            for c in range(NFI):
                p.op("tensor", lambda e, o=banks[b][:, 0:256],
                     l=PX[:, c * T + tc * 128: c * T + tc * 128 + 128],
                     r=slot[:, c * 256:(c + 1) * 256],
                     st=(c == 0), sp=(c == NFI - 1):
                     e.matmul(o, l, r, start=st, stop=sp),
                     sem="PE" if c == NFI - 1 else None)
            pe_c = p.cnt["PE"]
            stg = STG[:, 3072 + (tc % 2) * 256: 3072 + (tc % 2) * 256 + 256]
            p.wait("scalar", "PE", pe_c)
            p.wait_key("scalar", ("A3stg", tc % 2))
            sc_c = p.op("scalar", lambda e, o=stg, i=banks[b][:, 0:256]:
                        e.activation(o, i, AF.Sigmoid), sem="SC")
            p.set_key(("bank", b), "SC", sc_c)
            p.wait("sync", "SC", sc_c)
            st_c = p.dma("sync", t["scr"]["g_scr"][tc, :, pan * 256:(pan + 1) * 256], stg, f"SS{tc % 2}")
            p.set_key(("A3stg", tc % 2), f"SS{tc % 2}", st_c)
        free()
    p.set_key("A3_stores", "FG", p.barrier())



def emit_sstart(p, t):
    CST, S, ccw = t["CST"], t["S"], t["ccw"]
    cc_out = t["cc_out"].ap()
    p.wait("scalar", "CC", p.cnt["CC"])
    p.wait_key("scalar", "A2_done_PE")
    aggv = CST[:, :].bitcast(F32)
    for h in range(H):
        par = h % 2
        agg = aggv[:, par * 512:par * 512 + 512]
        sem = "GL0" if par == 0 else "GL1"
        p.wait_key("scalar", ("aggslot", par))
        lc = None
        for r in range(4):
            lc = p.dma("scalar", agg[:, r * 128:(r + 1) * 128],
                       cc_out[r * H * 128 + h * 128: r * H * 128 + (h + 1) * 128, :], sem)
        p.wait("vector", sem, lc)
        p.wait_key("vector", ("Schain", h))
        dv = None
        for r in range(4):
            if dv is not None:
                p.wait("vector", "DV", dv)
            if r == 0:
                dv = p.op("vector", lambda e, o=S[:, h * 128:(h + 1) * 128],
                          i=agg[:, 0:128], s=ccw[:, h * 4:h * 4 + 1]:
                          e.tensor_scalar(o, i, s, None, op0=ALU.mult), sem="DV")
            else:
                dv = p.op("vector", lambda e, o=S[:, h * 128:(h + 1) * 128],
                          i=agg[:, r * 128:(r + 1) * 128],
                          s=ccw[:, h * 4 + r:h * 4 + r + 1],
                          m=S[:, h * 128:(h + 1) * 128]:
                          e.scalar_tensor_tensor(o, i, s, m, op0=ALU.mult, op1=ALU.add),
                          sem="DV")
        p.set_key(("aggslot", par), "DV", dv)
        p.set_key(("Schain", h), "DV", dv)
    if "S_dump" in t["tapped"]:
        p.wait("sync", "DV", p.cnt["DV"])
        p.dma("sync", t["tapped"]["S_dump"], t["S"][:, :], "ST")



def emit_B(p, t):
    banks, STG, PX, S = t["banks"], t["STG"], t["PX"], t["S"]
    bdv, ident, lnt = t["bdv"], t["ident"], t["lnt"]
    qscr, kscr, vscr, kdscr = (t["scr"][n] for n in ["q_scr", "k_scr", "v_scr", "kd_scr"])
    gscr, zscr, mscr = t["scr"]["g_scr"], t["scr"]["z_scr"], t["scr"]["qdm_scr"]

    attn = PX[:, 0:8192]
    gate_slot = [PX[:, 8192:16384], PX[:, 16384:24576]]
    zbf = PX[:, 24576:32768]
    gslots = [PX[:, 32768:40960], PX[:, 40960:49152]]
    qdbuf = [PX[:, 49152:50176], PX[:, 50176:51200]]
    sbfbuf = [PX[:, 51200:51712], PX[:, 51712:52224]]
    zstg = [PX[:, 52224:52736], PX[:, 52736:53248]]
    sdbuf = [STG[:, 0:2048], STG[:, 2048:4096]]

    p.wait_key("sync", "A3_stores")
    p.wait_key("sync", "A2_stores")
    p.seed_key("BattnSC", "PE")
    p.seed_key("BattnDV", "PE")
    for i in range(2):
        p.seed_key(("Bzb", i), "PE")
        p.seed_key(("Bgate", i), "PE")
        p.seed_key(("Bslot", i), "PE")
        p.seed_key(("Bsdbuf", i), "PE")
        p.seed_key(("Bqdbuf", i), "PE")
        p.seed_key(("Bsbfbuf", i), "PE")
        p.seed_key(("Bk23", i), "SC")
        p.seed_key(("Bk01", i), "DV")
        p.seed_key(("Bzstg", i), f"SS{i}")
    for u in range(4):
        p.seed_key(("BbankS", u), "SC")
    p.seed_key("Baggdone", "DV")

    def stage1(b, g):
        par = (b * 8 + g) % 2
        qsem = "QL0" if par == 0 else "QL1"
        slot = gslots[par]
        p.wait_key("sync", ("Bslot", par))
        if b == 0 and g < 2:
            kk = p.keys.get("Baggdone")
            p.wait("sync", kk[0], kk[1])
        p.dma("sync", slot[:, 0:1024],
              qscr[b, 4 * g:4 * g + 4].rearrange("h q t -> q h t"), qsem)
        p.dma("sync", slot[:, 1024:2048],
              kscr[b, 4 * g:4 * g + 4].rearrange("h q t -> q h t"), qsem)
        p.dma("sync", slot[:, 2048:3072],
              kdscr[b, 4 * g:4 * g + 4].rearrange("h q t -> q h t"), qsem)
        p.dma("sync", slot[:, 3072:3584],
              vscr[2 * b, 2 * g:2 * g + 2].rearrange("n q t -> q n t"), qsem)
        p.dma("sync", slot[:, 3584:4096],
              vscr[2 * b + 1, 2 * g:2 * g + 2].rearrange("n q t -> q n t"), qsem)
        lcnt = p.dma("sync", slot[:, 4096:7168],
                     mscr[:, (4 * g) * 768:(4 * g + 4) * 768], qsem)
        p.wait("tensor", qsem, lcnt)
        for u in range(4):
            p.wait_key("tensor", ("BbankS", u))
        pe_s = None
        for u in range(4):
            for jc in range(2):
                pe_s = p.op("tensor", lambda e,
                            o=banks[4 + u][:, jc * 256:(jc + 1) * 256],
                            l=slot[:, 1024 + u * 256 + jc * 128: 1024 + u * 256 + (jc + 1) * 128],
                            r=slot[:, u * 256:(u + 1) * 256]:
                            e.matmul(o, l, r, start=True, stop=True),
                            sem="PE" if (u == 3 and jc == 1) else None)
        sd = sdbuf[par]
        p.wait("vector", "PE", pe_s)
        p.wait_key("vector", ("Bsdbuf", par))
        dv_sd = {}
        for u in range(4):
            dv_sd[u] = p.op("vector", lambda e, o=sd[:, u * 512:(u + 1) * 512],
                            i=banks[4 + u][:, 0:512],
                            m=slot[:, 4096 + u * 768: 4096 + u * 768 + 512]:
                            e.tensor_tensor(o, i, m, op=ALU.mult), sem="DV")
            p.set_key(("BbankS", u), "DV", dv_sd[u])
        qd = qdbuf[par]
        p.wait_key("vector", ("Bqdbuf", par))
        dv_qd = None
        for u in range(4):
            dv_qd = p.op("vector", lambda e, o=qd[:, u * 256:(u + 1) * 256],
                         i=slot[:, u * 256:(u + 1) * 256],
                         m=slot[:, 4096 + u * 768 + 512: 4096 + (u + 1) * 768]:
                         e.tensor_tensor(o, i, m, op=ALU.mult),
                         sem="DV" if u == 3 else None)
        sbf = sbfbuf[par]
        for u in range(4):
            kk = p.keys.get(("Schain", 4 * g + u))
            if kk is not None:
                p.wait("scalar", kk[0], kk[1])
        p.wait_key("scalar", ("Bsbfbuf", par))
        sc_b = p.op("scalar", lambda e, o=sbf[:, :],
                    i=S[:, (4 * g) * 128:(4 * g + 4) * 128]: e.copy(o, i), sem="SC")
        return dict(b=b, g=g, par=par, slot=slot, sd=sd, qd=qd, sbf=sbf,
                    dv_sd=dv_sd[3], dv_qd=dv_qd, sc_b=sc_b)

    def stage2(r):
        b, g, par, slot = r["b"], r["g"], r["par"], r["slot"]
        sd, qd, sbf = r["sd"], r["qd"], r["sbf"]
        gpar = (b * 8 + g) % 2
        p.wait("tensor", "DV", r["dv_qd"])
        p.wait("tensor", "DV", r["dv_sd"])
        p.wait("tensor", "SC", r["sc_b"])
        p.wait_key("tensor", ("Bk23", 0))
        p.wait_key("tensor", ("Bk23", 1))
        pe_P = None
        for u in range(4):
            pb = banks[2 + u // 2]
            base = (u % 2) * 256
            for ic in range(2):
                o = pb[:, base + ic * 128: base + (ic + 1) * 128]
                p.op("tensor", lambda e, o=o,
                     l=sd[:, u * 512 + ic * 128: u * 512 + ic * 128 + 128],
                     v=slot[:, 3072 + u * 128: 3072 + (u + 1) * 128]:
                     e.matmul(o, l, v, start=True, stop=False))
                p.op("tensor", lambda e, o=o,
                     l=sd[:, u * 512 + 256 + ic * 128: u * 512 + 256 + ic * 128 + 128],
                     v=slot[:, 3584 + u * 128: 3584 + (u + 1) * 128]:
                     e.matmul(o, l, v, start=False, stop=False))
                pe_P = p.op("tensor", lambda e, o=o,
                            l=qd[:, u * 256 + ic * 128: u * 256 + (ic + 1) * 128],
                            v=sbf[:, u * 128:(u + 1) * 128]:
                            e.matmul(o, l, v, start=False, stop=True),
                            sem="PE" if (u == 3 and ic == 1) else None)
        p.set_key(("Bsdbuf", par), "PE", pe_P)
        p.set_key(("Bqdbuf", par), "PE", pe_P)
        p.set_key(("Bsbfbuf", par), "PE", pe_P)
        cb = banks[gpar]
        p.wait_key("tensor", ("Bk01", gpar))
        pe_C = None
        for u in range(4):
            for jc in range(2):
                pe_C = p.op("tensor", lambda e, o=cb[:, u * 128:(u + 1) * 128],
                            l=slot[:, 2048 + u * 256 + jc * 128: 2048 + u * 256 + (jc + 1) * 128],
                            v=slot[:, 3072 + jc * 512 + u * 128: 3072 + jc * 512 + (u + 1) * 128],
                            st=(jc == 0), sp=(jc == 1):
                            e.matmul(o, l, v, start=st, stop=sp),
                            sem="PE" if (u == 3 and jc == 1) else None)
        p.set_key(("Bslot", par), "PE", pe_C)
        p.wait("scalar", "PE", pe_P)
        p.wait_key("scalar", "BattnSC")
        p.wait_key("scalar", "BattnDV")
        sc = None
        for u in range(4):
            pb = banks[2 + u // 2]
            base = (u % 2) * 256
            src2 = pb[:, base:base + 256].rearrange("q (ic f) -> q ic f", ic=2)
            dst = attn.rearrange("q (ic w) -> q ic w", ic=2)[:, :, (4 * g + u) * 128:(4 * g + u + 1) * 128]
            sc = p.op("scalar", lambda e, o=dst, i=src2: e.copy(o, i), sem="SC")
            if u % 2 == 1:
                p.set_key(("Bk23", u // 2), "SC", sc)
        p.set_key(("Battnw", b), "SC", sc)
        p.wait("vector", "PE", pe_C)
        p.wait("vector", "SC", r["sc_b"])
        dvs = None
        for u in range(4):
            h = 4 * g + u
            kk = p.keys.get(("Schain", h))
            p.wait("vector", kk[0], kk[1])
            dvs = p.op("vector", lambda e, o=S[:, h * 128:(h + 1) * 128],
                       s=bdv[:, h:h + 1], c=cb[:, u * 128:(u + 1) * 128]:
                       e.scalar_tensor_tensor(o, o, s, c, op0=ALU.mult, op1=ALU.add),
                       sem="DV")
            p.set_key(("Schain", h), "DV", dvs)
        p.set_key(("Bk01", gpar), "DV", dvs)

    prev = None
    for b in range(NB):
        gsem = "GL0" if b % 2 == 0 else "GL1"
        gslot = gate_slot[b % 2]
        p.wait_key("sync", ("Bgate", b % 2))
        if b < 2:
            kk = p.keys.get("Baggdone")
            p.wait("sync", kk[0], kk[1])
        p.dma("sync", gslot[:, 0:4096], gscr[2 * b, :, :], gsem)
        g_c = p.dma("sync", gslot[:, 4096:8192], gscr[2 * b + 1, :, :], gsem)
        p.set_key(("Bgater", b % 2), gsem, g_c)

        for g in range(8):
            rec = stage1(b, g)
            if prev is not None:
                stage2(prev)
            prev = rec
        stage2(prev)
        prev = None

        p.wait_key("vector", ("Bgater", b % 2))
        lastw = p.keys.get(("Battnw", b))
        for ic in range(2):
            att = attn[:, ic * 4096:(ic + 1) * 4096]
            zb = zbf[:, ic * 4096:(ic + 1) * 4096]
            p.wait("scalar", lastw[0], lastw[1])
            p.wait_key("scalar", ("Bzb", ic))
            ssq = lnt[:, 32 + ic:33 + ic]
            sc1 = p.op("scalar", lambda e, o=zb, i=att, a=ssq:
                       e.activation(o, i, AF.Square, accum_out=a), sem="SC")
            p.wait("scalar", "SC", sc1)
            lnv = lnt[:, 34 + ic:35 + ic]
            sc2 = p.op("scalar", lambda e, o=lnv, i=ssq:
                       e.activation(o, i, AF.Ln, scale=1.0 / D, bias=t["epsc"][:, 0:1]), sem="SC")
            p.wait("scalar", "SC", sc2)
            rstd = lnt[:, 36 + ic:37 + ic]
            sc3 = p.op("scalar", lambda e, o=rstd, i=lnv:
                       e.activation(o, i, AF.Exp, scale=-0.5), sem="SC")
            gat = gate_slot[b % 2][:, ic * 4096:(ic + 1) * 4096]
            p.wait("vector", "SC", sc3)
            dv1 = p.op("vector", lambda e, o=zb, i=att, s=rstd, g2=gat:
                       e.scalar_tensor_tensor(o, i, s, g2, op0=ALU.mult, op1=ALU.mult),
                       sem="DV")
            if ic == 1:
                p.set_key(("Bgate", b % 2), "DV", dv1)
                p.set_key("BattnDV", "DV", dv1)
            tc = 2 * b + ic
            for grp in range(8):
                tbk = banks[grp % 2].ap().bitcast(BF16)
                p.wait("tensor", "DV", dv1)
                p.wait_key("tensor", ("Bk01", grp % 2))
                for j in range(4):
                    c = grp * 4 + j
                    p.op("tensor", lambda e, o=tbk[:, j * 128:(j + 1) * 128],
                         i=zb[:, c * 128:(c + 1) * 128], idt=ident[:, :]:
                         e.transpose(o, i, idt),
                         sem="PE" if j == 3 else None)
                pe_t = p.cnt["PE"]
                stg = zstg[grp % 2]
                p.wait("vector", "PE", pe_t)
                p.wait_key("vector", ("Bzstg", grp % 2))
                dv_e = p.op("vector", lambda e, o=stg, i=tbk[:, 0:512]:
                            e.tensor_copy(o, i), sem="DV")
                p.set_key(("Bk01", grp % 2), "DV", dv_e)
                p.wait("sync", "DV", dv_e)
                st_c = p.dma("sync", zscr[:, tc * 4096 + grp * 512: tc * 4096 + (grp + 1) * 512],
                             stg, f"SS{grp % 2}")
                p.set_key(("Bzstg", grp % 2), f"SS{grp % 2}", st_c)
            p.set_key(("Bzb", ic), "PE", p.cnt["PE"])
        p.set_key("BattnSC", "SC", p.cnt["SC"])

    if "SB_dump" in t["tapped"]:
        p.wait("sync", "DV", p.cnt["DV"])
        p.dma("sync", t["tapped"]["SB_dump"], S[:, :], "ST")
    p.set_key("B_stores", "FG", p.barrier())
    p.set_key("B_PE_done", "PE", p.cnt["PE"])



def emit_C(p, t):
    ext, PX, STG, banks = t["ext"], t["PX"], t["STG"], t["banks"]
    load, free = p._slab_load, p._slab_free
    zscr, plscr = t["scr"]["z_scr"], t["scr"]["pl1_scr"]

    p.wait_key("gpsimd", "B_stores")
    p.wait_key("gpsimd", "B_PE_done")
    p.wait("gpsimd", "DV", p.cnt["DV"])
    p.wait("gpsimd", "SC", p.cnt["SC"])
    zl = None
    for i in range(8):
        w = NTC * D // 8
        zl = p.dma("gpsimd", PX[:, i * w:(i + 1) * w], zscr[:, i * w:(i + 1) * w], "LD")
    p.set_key("z_resident", "LD", zl)

    xres = STG[:, 0:2048].bitcast(F32)
    stg_o = STG[:, 2048:4096].bitcast(F32)
    for par in range(2):
        p.seed_key(("Cxs", par), "PE")
        p.seed_key(("Cstg", par), "PE")
        p.seed_key(("Css", par), f"SS{2 + par}")

    nwc = t["nwc"]
    for s in range(16):
        src = ext["wo"][:, s * NFI * 256:(s + 1) * NFI * 256]
        slot, lsem, lcnt = load(src, NFI * 256)
        p.wait("vector", lsem, lcnt)
        p.wait_key("vector", ("Cnw", 0))
        dvw = None
        for c in range(NFI):
            dvw = p.op("vector", lambda e, o=slot[:, c * 256:(c + 1) * 256],
                       i=slot[:, c * 256:(c + 1) * 256], sc=nwc[:, c:c + 1]:
                       e.tensor_scalar(o, i, sc, None, op0=ALU.mult),
                       sem="DV" if c == NFI - 1 else None)
        p.wait("tensor", "DV", dvw)
        if s == 0:
            p.wait_key("tensor", "z_resident")
        for tc in range(NTC):
            par = tc % 2
            xsem = "XL0" if par == 0 else "XL1"
            p.wait_key("scalar", ("Cxs", par))
            xc = p.dma("scalar", xres[:, par * 256:(par + 1) * 256],
                       ext["x_tm"][tc * 128:(tc + 1) * 128, s * 256:(s + 1) * 256], xsem)
            b = tc % 8
            p.wait_key("tensor", ("bank", b))
            for c in range(NFI):
                p.op("tensor", lambda e, o=banks[b][:, 0:256],
                     l=PX[:, tc * 4096 + c * 128: tc * 4096 + (c + 1) * 128],
                     r=slot[:, c * 256:(c + 1) * 256],
                     st=(c == 0), sp=(c == NFI - 1):
                     e.matmul(o, l, r, start=st, stop=sp),
                     sem="PE" if c == NFI - 1 else None)
            pe_c = p.cnt["PE"]
            p.wait("vector", "PE", pe_c)
            p.wait("vector", xsem, xc)
            p.wait_key("vector", ("Cstg", par))
            dv_c = p.op("vector", lambda e, o=stg_o[:, par * 256:(par + 1) * 256],
                        i=banks[b][:, 0:256], x=xres[:, par * 256:(par + 1) * 256]:
                        e.tensor_tensor(o, i, x, op=ALU.add), sem="DV")
            p.set_key(("bank", b), "DV", dv_c)
            p.set_key(("Cxs", par), "DV", dv_c)
            p.wait("sync", "DV", dv_c)
            p.wait_key("sync", ("Css", par))
            st_c = p.dma("sync", plscr[tc, :, s * 256:(s + 1) * 256],
                         stg_o[:, par * 256:(par + 1) * 256], f"SS{2 + par}")
            p.set_key(("Cstg", par), f"SS{2 + par}", st_c)
        free()
        p.set_key(("Cnw", 0), "PE", p.cnt["PE"])
    p.set_key("C_stores", "FG", p.barrier())
    p.set_key("C_PE_done", "PE", p.cnt["PE"])



def emit_LN1(p, t):
    PX, STG, banks, ident, lnt = t["PX"], t["STG"], t["banks"], t["ident"], t["lnt"]
    plscr, x1scr, x1fm = t["scr"]["pl1_scr"], t["scr"]["x1_scr"], t["scr"]["x1fm_scr"]
    row = [PX[:, 0:8192].bitcast(F32), PX[:, 8192:16384].bitcast(F32)]
    x1bf = [PX[:, 16384:20480], PX[:, 20480:24576]]

    cs = p.keys.get("C_stores")
    p.seed_key(("L1row", 0), "PE")
    p.seed_key(("L1row", 1), "PE")
    p.seed_key(("L1x1", 0), "PE")
    p.seed_key(("L1x1", 1), "PE")
    for i in range(2):
        p.seed_key(("L1stg", i), f"SS{i}")
        p.seed_key(("L1xrs", i), f"SS{2 + i}")

    for tc in range(NTC):
        par = tc % 2
        xsem = "XL0" if par == 0 else "XL1"
        p.wait_key("scalar", ("L1row", par))
        p.wait("scalar", cs[0], cs[1])
        rc_ = p.dma("scalar", row[par][:, :], plscr[tc, :, :], xsem)
        p.wait("vector", xsem, rc_)
        for i in range(8):
            p.op("vector", lambda e, o=lnt[:, 6 * i:6 * (i + 1)],
                 i2=row[par][:, i * 512:(i + 1) * 512]:
                 e.bn_stats(o, i2), sem="DV" if i == 7 else None)
        p.wait("vector", "DV", p.cnt["DV"])
        dva = p.op("vector", lambda e, o=lnt[:, 48:50], i=lnt[:, 0:48]:
                   e.bn_aggr(o, i), sem="DV")
        p.wait("scalar", "DV", dva)
        s1 = p.op("scalar", lambda e, o=lnt[:, 50:51], i=lnt[:, 49:50]:
                  e.activation(o, i, AF.Ln, bias=t["epsc"][:, 0:1]), sem="SC")
        p.wait("scalar", "SC", s1)
        s2 = p.op("scalar", lambda e, o=lnt[:, 51:52], i=lnt[:, 50:51]:
                  e.activation(o, i, AF.Exp, scale=-0.5), sem="SC")
        p.wait("vector", "SC", s2)
        dvn = p.op("vector", lambda e, o=row[par][:, :], i=row[par][:, :],
                   m=lnt[:, 48:49], r=lnt[:, 51:52]:
                   e.tensor_scalar(o, i, m, r, op0=ALU.subtract, op1=ALU.mult), sem="DV")
        p.wait("sync", "DV", dvn)
        p.wait_key("sync", ("L1xrs", par))
        st1 = p.dma("sync", x1scr[tc, :, :], row[par][:, :], f"SS{2 + par}")
        p.set_key(("L1xrs", par), f"SS{2 + par}", st1)
        p.wait("vector", "DV", dvn)
        p.wait_key("vector", ("L1x1", par))
        dvb = p.op("vector", lambda e, o=x1bf[par][:, :], i=row[par][:, :]:
                   e.tensor_copy(o, i), sem="DV")
        p.set_key(("L1row", par), "DV", dvb)
        for grp in range(8):
            tbk = banks[grp % 2].ap().bitcast(BF16)
            p.wait("tensor", "DV", dvb)
            p.wait_key("tensor", ("L1tb", grp % 2))
            if tc == 0 and grp < 2:
                p.wait_key("tensor", ("Bk01", grp % 2))
            for j in range(4):
                c = grp * 4 + j
                p.op("tensor", lambda e, o=tbk[:, j * 128:(j + 1) * 128],
                     i=x1bf[par][:, c * 128:(c + 1) * 128], idt=ident[:, :]:
                     e.transpose(o, i, idt),
                     sem="PE" if j == 3 else None)
            pe_t = p.cnt["PE"]
            stg = STG[:, (grp % 2) * 512:(grp % 2 + 1) * 512]
            p.wait("scalar", "PE", pe_t)
            p.wait_key("scalar", ("L1stg", grp % 2))
            sc_e = p.op("scalar", lambda e, o=stg, i=tbk[:, 0:512]: e.copy(o, i), sem="SC")
            p.set_key(("L1tb", grp % 2), "SC", sc_e)
            p.wait("sync", "SC", sc_e)
            st_c = p.dma("sync", x1fm[:, tc * 4096 + grp * 512: tc * 4096 + (grp + 1) * 512],
                         stg, f"SS{grp % 2}")
            p.set_key(("L1stg", grp % 2), f"SS{grp % 2}", st_c)
        p.set_key(("L1x1", par), "PE", p.cnt["PE"])
        if tc % 4 == 3:
            p.set_key(("L1tc", tc), "FG", p.barrier())
    p.set_key("L1_stores", "FG", p.cnt["FG"])



def emit_D1(p, t):
    ext, PX, banks = t["ext"], t["PX"], t["banks"]
    x1fm = t["scr"]["x1fm_scr"]
    load, free = p._slab_load, p._slab_free

    xsl = [PX[:, 28672:45056], PX[:, 0:16384]]
    Pfm = PX[:, 49152:65536]
    p.seed_key("D1_pfm_gate", "PE")
    for par in range(2):
        p.seed_key(("D1x", par), "PE")

    for k in range(4):
        par = k % 2
        xsem = "GL0" if par == 0 else "GL1"
        p.wait_key("scalar", ("D1x", par))
        kk = p.keys.get(("L1tc", 4 * k + 3))
        p.wait("scalar", kk[0], kk[1])
        if k == 1:
            p.wait_key("scalar", "L1_stores")
        xc = p.dma("scalar", xsl[par][:, :], x1fm[:, k * 16384:(k + 1) * 16384], xsem)
        for rcn in range(8):
            src2 = ext["u1"][:, rcn * NFI * 128:(rcn + 1) * NFI * 128]
            slot, lsem, lcnt = load(src2, NFI * 128)
            b = rcn % 8
            p.wait_key("tensor", ("bank", b))
            p.wait("tensor", xsem, xc)
            p.wait("tensor", lsem, lcnt)
            for c in range(NFI):
                rhs = xsl[par].rearrange("q (tc cc j) -> q tc cc j", tc=4, cc=NFI)[:, :, c, :]
                p.op("tensor", lambda e, o=banks[b][:, :],
                     l=slot[:, c * 128:(c + 1) * 128], r=rhs,
                     st=(c == 0), sp=(c == NFI - 1):
                     e.matmul(o, l, r, start=st, stop=sp),
                     sem="PE" if c == NFI - 1 else None)
            pe_c = p.cnt["PE"]
            free()
            p.wait("vector", "PE", pe_c)
            if k == 0 and rcn == 0:
                p.wait_key("vector", "D1_pfm_gate")
            dv_c = p.op("vector", lambda e,
                        o=Pfm[:, rcn * T + k * 512: rcn * T + (k + 1) * 512],
                        i=banks[b][:, :]: e.tensor_copy(o, i), sem="DV")
            p.set_key(("bank", b), "DV", dv_c)
            if rcn == 7:
                p.set_key(("D1x", par), "DV", dv_c)
    p.set_key("Pfm_ready", "DV", p.cnt["DV"])



def emit_D23(p, t):
    ext, PX, STG, banks = t["ext"], t["PX"], t["STG"], t["banks"]
    b1c = t["b1c"]
    load, free = p._slab_load, p._slab_free
    Pfm = PX[:, 49152:65536]
    act = [PX[:, i * 2048:(i + 1) * 2048] for i in range(8)]
    Z = PX[:, 16384:49152].bitcast(F32)
    silug = STG[:, 0:2048]

    pw = p.keys.get("Pfm_ready")
    for tt in range(4):
        p.seed_key(("D2sg", tt), "ST")
    for i in range(8):
        p.seed_key(("D2act", i), "PE")
    p.seed_key("D3z_gate", "PE")

    first_group = True
    idx = 0

    def h_chunk(fcpos, is_gate, act_slot):
        nonlocal idx
        src = ext["v1"][:, fcpos * 8 * 128:(fcpos + 1) * 8 * 128]
        slot, lsem, lcnt = load(src, 8 * 128)
        p.wait("tensor", lsem, lcnt)
        if idx == 0:
            p.wait("tensor", pw[0], pw[1])
        idx += 1
        for tt in range(4):
            b = tt % 4
            p.wait_key("tensor", ("bank", b))
            for rcn in range(8):
                p.op("tensor", lambda e, o=banks[b][:, :],
                     l=slot[:, rcn * 128:(rcn + 1) * 128],
                     r=Pfm[:, rcn * T + tt * 512: rcn * T + (tt + 1) * 512],
                     st=(rcn == 0), sp=(rcn == 7):
                     e.matmul(o, l, r, start=st, stop=sp),
                     sem="PE" if rcn == 7 else None)
            pe_c = p.cnt["PE"]
            if is_gate:
                p.wait("scalar", "PE", pe_c)
                p.wait_key("scalar", ("D2sg", tt))
                sc_c = p.op("scalar", lambda e, o=silug[:, tt * 512:(tt + 1) * 512],
                            i=banks[b][:, :], bb=b1c[:, fcpos:fcpos + 1]:
                            e.activation(o, i, AF.Silu, bias=bb), sem="SC")
                p.set_key(("bank", b), "SC", sc_c)
                p.set_key(("D2sgw", tt), "SC", sc_c)
            else:
                p.wait("vector", "PE", pe_c)
                kk = p.keys.get(("D2sgw", tt))
                p.wait("vector", kk[0], kk[1])
                p.wait_key("vector", ("D2act", act_slot))
                dv_c = p.op("vector", lambda e,
                            o=act[act_slot][:, tt * 512:(tt + 1) * 512],
                            i=banks[b][:, :], bb=b1c[:, fcpos:fcpos + 1],
                            g=silug[:, tt * 512:(tt + 1) * 512]:
                            e.scalar_tensor_tensor(o, i, bb, g, op0=ALU.add, op1=ALU.mult),
                            sem="DV")
                p.set_key(("bank", b), "DV", dv_c)
                p.set_key(("D2sg", tt), "DV", dv_c)
                if tt == 3:
                    p.set_key(("actready", act_slot), "DV", dv_c)
        free()

    for g0 in range(0, GCH, 4):
        gsz = min(4, GCH - g0)
        for j in range(gsz):
            fc = g0 + j
            h_chunk(2 * fc, True, fc % 8)
            h_chunk(2 * fc + 1, False, fc % 8)
        src = ext["u2"][:, g0 * RANK:(g0 + gsz) * RANK]
        slot, lsem, lcnt = load(src, gsz * RANK)
        p.wait("tensor", lsem, lcnt)
        kk = p.keys.get(("actready", (g0 + gsz - 1) % 8))
        p.wait("tensor", kk[0], kk[1])
        if first_group:
            p.wait_key("vector", "D3z_gate")
        for rcn in range(8):
            for tt in range(4):
                b = 4 + (rcn * 4 + tt) % 4
                p.wait_key("tensor", ("bank", b))
                for j in range(gsz):
                    p.op("tensor", lambda e, o=banks[b][:, :],
                         l=slot[:, j * RANK + rcn * 128: j * RANK + (rcn + 1) * 128],
                         r=act[(g0 + j) % 8][:, tt * 512:(tt + 1) * 512],
                         st=(j == 0), sp=(j == gsz - 1):
                         e.matmul(o, l, r, start=st, stop=sp),
                         sem="PE" if j == gsz - 1 else None)
                pe_c = p.cnt["PE"]
                p.wait("vector", "PE", pe_c)
                zsl = Z[:, rcn * T + tt * 512: rcn * T + (tt + 1) * 512]
                if first_group:
                    dv_c = p.op("vector", lambda e, o=zsl, i=banks[b][:, :]:
                                e.tensor_copy(o, i), sem="DV")
                else:
                    p.wait_key("vector", ("Zsl", (rcn, tt)))
                    dv_c = p.op("vector", lambda e, o=zsl, i=banks[b][:, :], z=zsl:
                                e.tensor_tensor(o, i, z, op=ALU.add), sem="DV")
                p.set_key(("bank", b), "DV", dv_c)
                p.set_key(("Zsl", (rcn, tt)), "DV", dv_c)
        grp_pe = p.cnt["PE"]
        for j in range(gsz):
            p.set_key(("D2act", (g0 + j) % 8), "PE", grp_pe)
        first_group = False
        free()
    p.set_key("Z_ready", "DV", p.cnt["DV"])
    p.set_key("Pfm_dead", "PE", p.cnt["PE"])



def emit_D4(p, t):
    ext, PX, STG, banks = t["ext"], t["PX"], t["STG"], t["banks"]
    ones, lnt = t["ones"], t["lnt"]
    x1scr, pl2scr, OUT = t["scr"]["x1_scr"], t["scr"]["pl2_scr"], t["OUT"]
    load, free = p._slab_load, p._slab_free
    Z = PX[:, 16384:49152].bitcast(F32)
    Zbf = PX[:, 0:16384]
    x1res = STG[:, 0:2048].bitcast(F32)
    stg_o = STG[:, 2048:4096].bitcast(F32)

    zr = p.keys.get("Z_ready")
    pfd = p.keys.get("Pfm_dead")
    p.wait("vector", zr[0], zr[1])
    p.wait("vector", pfd[0], pfd[1])
    dvz = None
    for rcn in range(8):
        dvz = p.op("vector", lambda e, o=Zbf[:, rcn * T:(rcn + 1) * T],
                   i=Z[:, rcn * T:(rcn + 1) * T]: e.tensor_copy(o, i), sem="DV")
    p.set_key("Zbf_ready", "DV", dvz)

    for par in range(2):
        p.seed_key(("D4x", par), "PE")
        p.seed_key(("D4stg", par), f"SS{par}")

    lw = p.keys.get("L1_stores")
    b2t = t["b2"]
    for s in range(8):
        src2 = ext["v2"][:, s * 4096:(s + 1) * 4096]
        slot, lsem, lcnt = load(src2, 4096)
        bpar = s % 2
        bsem = "B20" if bpar == 0 else "B21"
        p.wait_key("gpsimd", ("D4b2", bpar))
        b2c = p.dma("gpsimd", b2t[0:1, bpar * 512:(bpar + 1) * 512],
                    ext["b2"][0:1, s * 512:(s + 1) * 512], bsem)
        p.wait("tensor", lsem, lcnt)
        p.wait("tensor", bsem, b2c)
        if s == 0:
            p.wait_key("tensor", "Zbf_ready")
        for tc in range(NTC):
            par = tc % 2
            xsem = "XL0" if par == 0 else "XL1"
            p.wait_key("scalar", ("D4x", par))
            if s == 0 and tc < 2:
                p.wait("scalar", lw[0], lw[1])
            xc = p.dma("scalar", x1res[:, par * 512:(par + 1) * 512],
                       x1scr[tc, :, s * 512:(s + 1) * 512], xsem)
            b = tc % 8
            p.wait_key("tensor", ("bank", b))
            for rcn in range(8):
                p.op("tensor", lambda e, o=banks[b][:, :],
                     l=Zbf[:, rcn * T + tc * 128: rcn * T + tc * 128 + 128],
                     r=slot[:, rcn * 512:(rcn + 1) * 512],
                     st=(rcn == 0), sp=False:
                     e.matmul(o, l, r, start=st, stop=sp))
            p.op("tensor", lambda e, o=banks[b][:, :],
                 l=ones[0:1, 0:128], r=b2t[0:1, bpar * 512:(bpar + 1) * 512]:
                 e.matmul(o, l, r, start=False, stop=True), sem="PE")
            pe_c = p.cnt["PE"]
            p.wait("vector", "PE", pe_c)
            p.wait("vector", xsem, xc)
            p.wait_key("vector", ("D4stg", par))
            dv_c = p.op("vector", lambda e, o=stg_o[:, par * 512:(par + 1) * 512],
                        i=banks[b][:, :], x=x1res[:, par * 512:(par + 1) * 512]:
                        e.tensor_tensor(o, i, x, op=ALU.add), sem="DV")
            p.set_key(("bank", b), "DV", dv_c)
            p.set_key(("D4x", par), "DV", dv_c)
            p.wait("sync", "DV", dv_c)
            st_c = p.dma("sync", pl2scr[tc, :, s * 512:(s + 1) * 512],
                         stg_o[:, par * 512:(par + 1) * 512], f"SS{par}")
            p.set_key(("D4stg", par), f"SS{par}", st_c)
        free()
        p.set_key(("D4b2", bpar), "PE", p.cnt["PE"])
    p.set_key("D4a_stores", "FG", p.barrier())

    row = [PX[:, 16384:24576].bitcast(F32), PX[:, 24576:32768].bitcast(F32)]
    d4s = p.keys.get("D4a_stores")
    p.seed_key(("L2row", 0), "DV")
    p.seed_key(("L2row", 1), "DV")
    for tc in range(NTC):
        par = tc % 2
        xsem = "XL0" if par == 0 else "XL1"
        p.wait_key("scalar", ("L2row", par))
        p.wait("scalar", d4s[0], d4s[1])
        rc_ = p.dma("scalar", row[par][:, :], pl2scr[tc, :, :], xsem)
        p.wait("vector", xsem, rc_)
        for i in range(8):
            p.op("vector", lambda e, o=lnt[:, 6 * i:6 * (i + 1)],
                 i2=row[par][:, i * 512:(i + 1) * 512]:
                 e.bn_stats(o, i2), sem="DV" if i == 7 else None)
        p.wait("vector", "DV", p.cnt["DV"])
        dva = p.op("vector", lambda e, o=lnt[:, 48:50], i=lnt[:, 0:48]:
                   e.bn_aggr(o, i), sem="DV")
        p.wait("scalar", "DV", dva)
        s1 = p.op("scalar", lambda e, o=lnt[:, 52:53], i=lnt[:, 49:50]:
                  e.activation(o, i, AF.Ln, bias=t["epsc"][:, 0:1]), sem="SC")
        p.wait("scalar", "SC", s1)
        s2 = p.op("scalar", lambda e, o=lnt[:, 53:54], i=lnt[:, 52:53]:
                  e.activation(o, i, AF.Exp, scale=-0.5), sem="SC")
        p.wait("vector", "SC", s2)
        dvn = p.op("vector", lambda e, o=row[par][:, :], i=row[par][:, :],
                   m=lnt[:, 48:49], r=lnt[:, 53:54]:
                   e.tensor_scalar(o, i, m, r, op0=ALU.subtract, op1=ALU.mult), sem="DV")
        p.wait("sync", "DV", dvn)
        st_c = p.dma("sync", OUT[tc * 128:(tc + 1) * 128, :], row[par][:, :], f"SS{2 + par}")
        p.set_key(("L2row", par), f"SS{2 + par}", st_c)


def emit_taps(p, t):
    if not t["tapped"]:
        return
    for s in ["ST", "SS0", "SS1", "SS2", "SS3"]:
        p.wait("sync", s, p.cnt[s])
    p.wait("sync", "DV", p.cnt["DV"])
    p.wait("sync", "SC", p.cnt["SC"])
    p.wait("sync", "PE", p.cnt["PE"])
    p.wait("sync", "GP", p.cnt["GP"])
    for name, dst in t["tapped"].items():
        if name in ("S_dump", "SB_dump"):
            continue
        p.dma("sync", dst, t["scr"][name], "ST")


def emit_final(p, t):
    for s in ["ST", "SS0", "SS1", "SS2", "SS3"]:
        p.wait("sync", s, p.cnt[s])
    p.wait("sync", "PE", p.cnt["PE"])
    p.wait("sync", "DV", p.cnt["DV"])
    p.wait("sync", "SC", p.cnt["SC"])
    p.wait("sync", "GP", p.cnt["GP"])
    p.wait("sync", "CC", p.cnt["CC"])
    p.wait("sync", "LD", p.cnt["LD"])
    p.wait("sync", "WL0", p.cnt["WL0"])
    p.wait("sync", "WL1", p.cnt["WL1"])
    p.wait("sync", "QL0", p.cnt["QL0"])
    p.wait("sync", "QL1", p.cnt["QL1"])
    p.wait("sync", "GL0", p.cnt["GL0"])
    p.wait("sync", "GL1", p.cnt["GL1"])
    p.wait("sync", "XL0", p.cnt["XL0"])
    p.wait("sync", "XL1", p.cnt["XL1"])



def kernel(**inputs):
    in_maps = pack_inputs(inputs)
    nc = build_program(taps=tuple(TAPS), stop_after=STOP_AFTER)
    res = run_bass_kernel_spmd(nc, in_maps, core_ids=list(range(NCORES)))
    out = np.concatenate([res.results[c]["out"] for c in range(NCORES)], axis=0)
    return out.reshape(BATCH, SEQ, D)


# version 11
# speedup vs baseline: 1.2195x; 1.0337x over previous
import numpy as np
from contextlib import ExitStack

import concourse.bass as bass
import concourse.mybir as mybir
from concourse.bass_utils import run_bass_kernel_spmd

F32 = mybir.dt.float32
BF16 = mybir.dt.bfloat16
I32 = mybir.dt.int32
AF = mybir.ActivationFunctionType
ALU = mybir.AluOpType

BATCH, SEQ, D = 2, 8192, 4096
H, HD, BS = 32, 128, 256
LAYER_IDX, N_LAYERS = 1, 32
EPS = 1e-5
INTER, RANK = 11008, 1024

NCORES = 8
T = 2048
NB = T // BS
NFI = D // 128
NTC = T // 128
GCH = INTER // 128
FCH = 2 * GCH

TAPS = []
STOP_AFTER = None


def _decays_np():
    base = 1.0 / 2.0 ** (8.0 / H)
    exponent = np.arange(H, dtype=np.float64) + 1.0
    factor = 1.0 - LAYER_IDX / (N_LAYERS - 1 + 1e-5) + 1e-5
    slope = (base ** exponent) * factor
    r = np.arange(BS, dtype=np.float64) + 1.0
    q_decay = np.exp(-slope[:, None] * r[None, :])
    k_decay = np.exp(-slope[:, None] * (BS - r)[None, :])
    diff = r[:, None] - r[None, :]
    diag = slope[:, None, None] * diff[None, :, :]
    diag_decay = np.where(diag >= 0, np.exp(-diag), 0.0)
    block_decay = np.exp(-slope * BS)
    return slope, q_decay, k_decay, diag_decay, block_decay


def _img(a2d):
    rows, cols = a2d.shape
    c = rows // 128
    return np.ascontiguousarray(
        a2d.reshape(c, 128, cols).transpose(1, 0, 2).reshape(128, c * cols))


def pack_inputs(inp):
    x = np.asarray(inp["L0_attn_hidden_states"], np.float32)
    qkv_w = np.asarray(inp["L0_attn_qkv_w"], np.float32)
    out_w = np.asarray(inp["L0_attn_out_w"], np.float32)
    gate_w = np.asarray(inp["L0_attn_gate_w"], np.float32)
    norm_w = np.asarray(inp["L0_attn_norm_w"], np.float32)
    U1 = np.asarray(inp["L0_ffn_U1"], np.float32)
    V1 = np.asarray(inp["L0_ffn_V1"], np.float32)
    U2 = np.asarray(inp["L0_ffn_U2"], np.float32)
    V2 = np.asarray(inp["L0_ffn_V2"], np.float32)
    b1 = np.asarray(inp["L0_ffn_b1"], np.float32)
    b2 = np.asarray(inp["L0_ffn_b2"], np.float32)

    qkvT = np.ascontiguousarray(qkv_w.T)
    gateT = np.ascontiguousarray(gate_w.T)
    outT = np.ascontiguousarray(out_w.T)

    qk_cols = []
    for h in range(H):
        qk_cols.append(qkvT[:, 384 * h:384 * h + 128])
        qk_cols.append(qkvT[:, 384 * h + 128:384 * h + 256])
    wqk = np.concatenate([_img(c) for c in qk_cols], axis=1)

    v_cols = []
    for pan in range(16):
        h0, h1 = 2 * pan, 2 * pan + 1
        v_cols.append(np.concatenate(
            [qkvT[:, 384 * h0 + 256:384 * h0 + 384],
             qkvT[:, 384 * h1 + 256:384 * h1 + 384]], axis=1))
    wv = np.concatenate([_img(c) for c in v_cols], axis=1)

    wg = np.concatenate([_img(gateT[:, 256 * s:256 * (s + 1)]) for s in range(16)], axis=1)
    wo = np.concatenate([_img(outT[:, 256 * s:256 * (s + 1)]) for s in range(16)], axis=1)

    u1 = np.concatenate([_img(U1[:, 128 * rc:128 * (rc + 1)]) for rc in range(8)], axis=1)

    v1_order = []
    for g in range(GCH):
        v1_order += [g, GCH + g]
    v1 = np.concatenate([_img(V1[:, 128 * fc:128 * (fc + 1)]) for fc in v1_order], axis=1)

    u2 = np.ascontiguousarray(
        U2.reshape(GCH, 128, RANK).transpose(1, 0, 2).reshape(128, GCH * RANK))

    v2 = np.concatenate([_img(V2[:, 512 * s:512 * (s + 1)]) for s in range(8)], axis=1)

    slope, q_decay, k_decay, diag_decay, block_decay = _decays_np()
    qdm = np.zeros((128, H * 768), np.float32)
    for h in range(H):
        for jc in range(2):
            qdm[:, h * 768 + jc * 256: h * 768 + (jc + 1) * 256] = \
                diag_decay[h, :, jc * 128:(jc + 1) * 128].T
        qdm[:, h * 768 + 512: h * 768 + 768] = q_decay[h][None, :]
    kdec = np.zeros((128, H * 2), np.float32)
    for h in range(H):
        for jc in range(2):
            kdec[:, h * 2 + jc] = k_decay[h, jc * 128:(jc + 1) * 128]
    bdv = np.tile(block_decay.reshape(1, H), (128, 1)).astype(np.float32)

    ccw_all = []
    for core in range(NCORES):
        g = core % 4
        w = np.zeros((H, 4), np.float64)
        for r in range(g):
            w[:, r] = block_decay ** (8.0 * (g - 1 - r))
        ccw_all.append(np.tile(w.reshape(1, H * 4), (128, 1)).astype(np.float32))

    nwc = np.ascontiguousarray(norm_w.reshape(NFI, 128).T)
    b1c = np.stack([b1[128 * fc:128 * (fc + 1)] for fc in v1_order], axis=1)
    b2_pack = b2.reshape(1, D)

    shared = dict(wqk=wqk, wv=wv, wg=wg, wo=wo, u1=u1, v1=v1, u2=u2, v2=v2,
                  qdm=qdm, kdec=kdec, bdv=bdv, nwc=nwc,
                  b1c=np.ascontiguousarray(b1c),
                  b2=np.ascontiguousarray(b2_pack))

    xr = x.reshape(BATCH * SEQ, D)
    in_maps = []
    for core in range(NCORES):
        sl = xr[core * T:(core + 1) * T]
        m = dict(shared)
        m["x_tm"] = np.ascontiguousarray(sl)
        m["x_fm"] = _img(np.ascontiguousarray(sl.T))
        m["ccw"] = ccw_all[core]
        in_maps.append(m)
    return in_maps



ENGINES = ("sync", "scalar", "vector", "tensor", "gpsimd")


class Prog:
    def __init__(self, nc, es):
        self.nc, self.es = nc, es
        self.ops = {e: [] for e in ENGINES}
        self.sem, self.cnt = {}, {}
        self.keys = {}

    def new_sem(self, name):
        self.sem[name] = self.es.enter_context(self.nc.semaphore(name))
        self.cnt[name] = 0

    def emit(self, eng, fn):
        self.ops[eng].append(fn)

    def wait(self, eng, name, val):
        if val is None or val <= 0:
            return
        h = self.sem[name]
        self.emit(eng, lambda e, h=h, v=val: e.wait_ge(h, v))

    def wait_key(self, eng, key):
        g = self.keys.get(key)
        if g is not None:
            self.wait(eng, g[0], g[1])

    def set_key(self, key, sem, cnt):
        self.keys[key] = (sem, cnt)

    def seed_key(self, key, sem):
        self.keys[key] = (sem, self.cnt[sem])

    def op(self, eng, build, sem=None, amt=1):
        if sem:
            c = self.cnt[sem] = self.cnt[sem] + amt
            h = self.sem[sem]
            self.emit(eng, lambda e, b=build, h=h, a=amt: b(e).then_inc(h, a))
            return c
        self.emit(eng, lambda e, b=build: b(e))
        return None

    def barrier(self):
        for s in ["ST", "SS0", "SS1", "SS2", "SS3"]:
            self.wait("sync", s, self.cnt[s])
        c = self.cnt["FG"] = self.cnt["FG"] + 1
        h = self.sem["FG"]
        self.emit("sync", lambda e, h=h: e.sem_inc(h, 1))
        return c

    def dma(self, eng, out, in_, sem, **kw):
        c = self.cnt[sem] = self.cnt[sem] + 16
        h = self.sem[sem]
        self.emit(eng, lambda e, o=out, i=in_, h=h, kw=kw:
                  e.dma_start(out=o, in_=i, **kw).then_inc(h, 16))
        return c


def build_program(taps=(), stop_after=None):
    nc = bass.Bass("TRN2", target_bir_lowering=False, debug=False)

    ext = {}
    for name, shape in [
        ("x_tm", (T, D)), ("x_fm", (128, NFI * T)),
        ("wqk", (128, 64 * NFI * 128)), ("wv", (128, 16 * NFI * 256)),
        ("wg", (128, 16 * NFI * 256)), ("wo", (128, 16 * NFI * 256)),
        ("u1", (128, 8 * NFI * 128)), ("v1", (128, FCH * 8 * 128)),
        ("u2", (128, GCH * RANK)), ("v2", (128, 8 * 8 * 512)),
        ("qdm", (128, H * 768)),
        ("kdec", (128, H * 2)), ("bdv", (128, H)), ("ccw", (128, H * 4)),
        ("nwc", (128, NFI)), ("b1c", (128, FCH)), ("b2", (1, D)),
    ]:
        ext[name] = nc.dram_tensor(name, list(shape), F32, kind="ExternalInput").ap()
    OUT = nc.dram_tensor("out", [T, D], F32, kind="ExternalOutput").ap()

    scr = {}
    scr["q_scr"] = nc.dram_tensor("q_scr", [NB, H, 128, 256], BF16).ap()
    scr["k_scr"] = nc.dram_tensor("k_scr", [NB, H, 128, 256], BF16).ap()
    scr["v_scr"] = nc.dram_tensor("v_scr", [NTC, 16, 128, 256], BF16).ap()
    scr["kd_scr"] = nc.dram_tensor("kd_scr", [NB, H, 128, 256], BF16).ap()
    scr["g_scr"] = nc.dram_tensor("g_scr", [NTC, 128, D], BF16).ap()
    scr["z_scr"] = nc.dram_tensor("z_scr", [128, NTC * D], BF16).ap()
    scr["pl1_scr"] = nc.dram_tensor("pl1_scr", [NTC, 128, D], F32).ap()
    scr["x1_scr"] = nc.dram_tensor("x1_scr", [NTC, 128, D], F32).ap()
    scr["x1fm_scr"] = nc.dram_tensor("x1fm_scr", [128, NTC * D], BF16).ap()
    scr["qdm_scr"] = nc.dram_tensor("qdm_scr", [128, H * 768], BF16).ap()
    scr["pl2_scr"] = nc.dram_tensor("pl2_scr", [NTC, 128, D], F32).ap()
    cc_in_t = nc.dram_tensor("cc_in", [H * 128, 128], F32)
    cc_out_t = nc.dram_tensor("cc_out", [4 * H * 128, 128], F32)

    tapped = {}
    for name in taps:
        if name in ("S_dump", "SB_dump"):
            tapped[name] = nc.dram_tensor("tap_" + name, [128, H * 128], F32,
                                          kind="ExternalOutput").ap()
            continue
        s = scr[name]
        tapped[name] = nc.dram_tensor("tap_" + name, list(s.shape), s.dtype,
                                      kind="ExternalOutput").ap()

    es = ExitStack()
    with es:
        t = {"ext": ext, "OUT": OUT, "scr": scr, "cc_in": cc_in_t,
             "cc_out": cc_out_t, "tapped": tapped}
        sb = lambda n, shape, dt: es.enter_context(nc.sbuf_tensor(n, list(shape), dt))
        t["PX"] = sb("PX", (128, 65536), BF16)
        t["WS"] = [sb("WS0", (128, 8192), BF16), sb("WS1", (128, 8192), BF16)]
        t["STG"] = sb("STG", (128, 4096), BF16)
        t["CST"] = sb("CST", (128, 3584), BF16)
        t["kdec"] = sb("kdecs", (128, H * 2), F32)
        t["bdv"] = sb("bdvs", (128, H), F32)
        t["ccw"] = sb("ccws", (128, H * 4), F32)
        t["nwc"] = sb("nwcs", (128, NFI), F32)
        t["b1c"] = sb("b1cs", (128, FCH), F32)
        t["b2"] = sb("b2s", (1, 1024), BF16)
        t["ones"] = sb("oness", (1, 512), BF16)
        t["epsc"] = sb("epscs", (128, 1), F32)
        t["ident"] = sb("idents", (128, 128), BF16)
        t["ident_i"] = sb("identis", (128, 128), I32)
        t["S"] = sb("Ss", (128, H * 128), F32)
        t["lnt"] = sb("lnts", (128, 64), F32)
        t["banks"] = [es.enter_context(nc.psum_tensor(f"bk{i}", [128, 512], F32))
                      for i in range(8)]

        p = Prog(nc, es)
        for s in ["LD", "WL0", "WL1", "PE", "DV", "SC", "GP", "ST",
                  "SS0", "SS1", "SS2", "SS3", "FG", "B20", "B21", "ML0", "ML1",
                  "QL0", "QL1", "GL0", "GL1", "XL0", "XL1", "CC"]:
            p.new_sem(s)

        phases = [emit_consts, emit_A1, emit_A3, emit_A2, emit_sstart,
                  emit_B, emit_C, emit_LN1, emit_D1, emit_D23, emit_D4]
        for ph in phases:
            ph(p, t)
            if stop_after is not None and ph.__name__ == stop_after:
                break
        emit_taps(p, t)
        emit_final(p, t)

        block = es.enter_context(nc.Block())

        @block.sync
        def _(e):
            for f in p.ops["sync"]:
                f(e)

        @block.scalar
        def _(e):
            for f in p.ops["scalar"]:
                f(e)

        @block.vector
        def _(e):
            for f in p.ops["vector"]:
                f(e)

        @block.tensor
        def _(e):
            for f in p.ops["tensor"]:
                f(e)

        @block.gpsimd
        def _(e):
            for f in p.ops["gpsimd"]:
                f(e)

    return nc



def emit_consts(p, t):
    ext = t["ext"]
    for name, dst in [("kdec", t["kdec"]), ("bdv", t["bdv"]), ("ccw", t["ccw"]),
                      ("nwc", t["nwc"]), ("b1c", t["b1c"])]:
        p.dma("gpsimd", dst[:, :], ext[name], "LD", max_dma_last_dim=8192)
    p.dma("gpsimd", t["scr"]["qdm_scr"], ext["qdm"], "LD", max_dma_last_dim=8192)
    PX = t["PX"]
    for i in range(4):
        w = NFI * T // 4
        p.dma("gpsimd", PX[:, i * w:(i + 1) * w], ext["x_fm"][:, i * w:(i + 1) * w],
              "LD", max_dma_last_dim=8192)
    p.op("gpsimd", lambda e: e.iota(t["ident_i"][:, :], pattern=[[1, 128]],
                                    base=0, channel_multiplier=-1), sem="GP")
    p.op("gpsimd", lambda e: e.memset(t["ones"][:, :], 1.0), sem="GP")
    p.op("gpsimd", lambda e: e.memset(t["epsc"][:, :], EPS), sem="GP")
    p.op("gpsimd", lambda e: e.memset(t["S"][:, :], 0.0), sem="GP")
    gp = p.cnt["GP"]
    p.wait("vector", "GP", gp)
    c = p.op("vector", lambda e: e.tensor_scalar(t["ident"][:, :], t["ident_i"][:, :],
                                                 0.0, None, op0=ALU.is_equal), sem="DV")
    p.set_key("ident", "DV", c)
    p.set_key("consts", "LD", p.cnt["LD"])
    p.set_key("zeroS", "GP", gp)


def _slab_loader(p, t):
    WS = t["WS"]
    state = {"idx": 0, "free_pe": []}

    def load(src_ap, ncols):
        i = state["idx"]
        slot = i % 2
        sem = "WL0" if slot == 0 else "WL1"
        if i >= 2:
            p.wait("gpsimd", "PE", state["free_pe"][i - 2])
        c = p.dma("gpsimd", WS[slot][:, 0:ncols], src_ap, sem, max_dma_last_dim=8192)
        state["idx"] = i + 1
        return WS[slot], sem, c

    def free():
        state["free_pe"].append(p.cnt["PE"])

    return load, free



def emit_A1(p, t):
    ext, PX, STG, banks = t["ext"], t["PX"], t["STG"], t["banks"]
    load, free = _slab_loader(p, t)
    p._slab_load, p._slab_free = load, free

    for s in range(64):
        src = ext["wqk"][:, s * NFI * 128:(s + 1) * NFI * 128]
        slot, lsem, lcnt = load(src, NFI * 128)
        p.wait("tensor", lsem, lcnt)
        if s == 0:
            p.wait_key("tensor", "consts")
        h, isq = s // 2, (s % 2 == 0)
        for tt in range(4):
            b = (s * 4 + tt) % 8
            p.wait_key("tensor", ("bank", b))
            for c in range(NFI):
                p.op("tensor", lambda e, o=banks[b][:, :],
                     l=slot[:, c * 128:(c + 1) * 128],
                     r=PX[:, c * T + tt * 512: c * T + tt * 512 + 512],
                     st=(c == 0), sp=(c == NFI - 1):
                     e.matmul(o, l, r, start=st, stop=sp),
                     sem="PE" if c == NFI - 1 else None)
            pe_c = p.cnt["PE"]
            stg = STG[:, (tt % 4) * 512:(tt % 4) * 512 + 512]
            p.wait("scalar", "PE", pe_c)
            p.wait_key("scalar", ("A1stg", tt % 4))
            sc_c = p.op("scalar", lambda e, o=stg, i=banks[b][:, :]:
                        e.activation(o, i, AF.Silu), sem="SC")
            p.set_key(("bank", b), "SC", sc_c)
            p.wait("sync", "SC", sc_c)
            scrq = t["scr"]["q_scr"] if isq else t["scr"]["k_scr"]
            dst = scrq[2 * tt:2 * tt + 2, h].rearrange("b q t -> q b t")
            st_c = p.dma("sync", dst, stg.rearrange("q (b t) -> q b t", b=2), f"SS{tt % 4}")
            p.set_key(("A1stg", tt % 4), f"SS{tt % 4}", st_c)
        free()

    for pan in range(16):
        src = ext["wv"][:, pan * NFI * 256:(pan + 1) * NFI * 256]
        slot, lsem, lcnt = load(src, NFI * 256)
        p.wait("tensor", lsem, lcnt)
        for tc in range(NTC):
            b = tc % 8
            p.wait_key("tensor", ("bank", b))
            for c in range(NFI):
                p.op("tensor", lambda e, o=banks[b][:, 0:256],
                     l=PX[:, c * T + tc * 128: c * T + tc * 128 + 128],
                     r=slot[:, c * 256:(c + 1) * 256],
                     st=(c == 0), sp=(c == NFI - 1):
                     e.matmul(o, l, r, start=st, stop=sp),
                     sem="PE" if c == NFI - 1 else None)
            pe_c = p.cnt["PE"]
            stg = STG[:, 2048 + (tc % 4) * 256: 2048 + (tc % 4) * 256 + 256]
            p.wait("scalar", "PE", pe_c)
            p.wait_key("scalar", ("Avstg", tc % 4))
            sc_c = p.op("scalar", lambda e, o=stg, i=banks[b][:, 0:256]:
                        e.activation(o, i, AF.Silu), sem="SC")
            p.set_key(("bank", b), "SC", sc_c)
            p.wait("sync", "SC", sc_c)
            st_c = p.dma("sync", t["scr"]["v_scr"][tc, pan, :, :], stg, f"SS{tc % 4}")
            p.set_key(("Avstg", tc % 4), f"SS{tc % 4}", st_c)
        free()

    p.set_key("A1_stores", "FG", p.barrier())



def emit_A2(p, t):
    banks, CST, STG, S = t["banks"], t["CST"], t["STG"], t["S"]
    kdec, bdv, ident = t["kdec"], t["bdv"], t["ident"]
    kscr, vscr, kdscr = t["scr"]["k_scr"], t["scr"]["v_scr"], t["scr"]["kd_scr"]

    p.wait_key("sync", "A1_stores")
    p.wait_key("tensor", "ident")
    p.wait_key("vector", "zeroS")
    slots = [CST[:, 0:2048], STG[:, 0:2048]]
    for par in range(2):
        p.seed_key(("A2slot", par), "PE")
        p.seed_key(("A2kd", par), "SC")
        p.seed_key(("A2tb", par), "SC")
        p.seed_key(("A2cb", par), "SC")
        p.seed_key(("A2kds", par), f"SS{2 + par}")

    for b in range(NB):
        for g in range(H // 4):
            par = (b * 8 + g) % 2
            qsem = "QL0" if par == 0 else "QL1"
            slot = slots[par]
            p.wait_key("sync", ("A2slot", par))
            p.dma("sync", slot[:, 0:1024],
                  kscr[b, 4 * g:4 * g + 4].rearrange("h q t -> q h t"), qsem)
            p.dma("sync", slot[:, 1024:1536],
                  vscr[2 * b, 2 * g:2 * g + 2].rearrange("n q t -> q n t"), qsem)
            lcnt = p.dma("sync", slot[:, 1536:2048],
                         vscr[2 * b + 1, 2 * g:2 * g + 2].rearrange("n q t -> q n t"), qsem)
            tb = banks[4 + par].ap().bitcast(BF16)
            p.wait("tensor", qsem, lcnt)
            p.wait_key("tensor", ("A2tb", par))
            pe_tr = None
            for u in range(4):
                for jc in range(2):
                    pe_tr = p.op("tensor", lambda e,
                                 o=tb[:, (u * 2 + jc) * 128:(u * 2 + jc + 1) * 128],
                                 i=slot[:, u * 256 + jc * 128: u * 256 + (jc + 1) * 128],
                                 idt=ident[:, :]: e.transpose(o, i, idt),
                                 sem="PE" if (u == 3 and jc == 1) else None)
            kdx = t["PX"][:, 49152 + par * 1024: 49152 + (par + 1) * 1024]
            p.wait("vector", "PE", pe_tr)
            p.wait_key("vector", ("A2kdx", par))
            dv1 = None
            for u in range(4):
                for jc in range(2):
                    h = 4 * g + u
                    dv1 = p.op("vector", lambda e,
                               o=kdx[:, (u * 2 + jc) * 128:(u * 2 + jc + 1) * 128],
                               i=tb[:, (u * 2 + jc) * 128:(u * 2 + jc + 1) * 128],
                               s=kdec[:, 2 * h + jc:2 * h + jc + 1]:
                               e.tensor_scalar(o, i, s, None, op0=ALU.mult),
                               sem="DV" if (u == 3 and jc == 1) else None)
            p.set_key(("A2tb", par), "DV", dv1)
            p.wait("sync", "DV", dv1)
            p.wait_key("sync", ("A2kds", par))
            st_c = p.dma("sync", kdscr[b, 4 * g:4 * g + 4].rearrange("h q t -> q h t"),
                         kdx.rearrange("q (h t) -> q h t", h=4), f"SS{2 + par}")
            p.set_key(("A2kds", par), f"SS{2 + par}", st_c)
            cb = banks[2 + par]
            p.wait("tensor", "DV", dv1)
            p.wait_key("tensor", ("A2cb", par))
            pe_c = None
            for u in range(4):
                for jc in range(2):
                    pe_c = p.op("tensor", lambda e, o=cb[:, u * 128:(u + 1) * 128],
                                l=kdx[:, (u * 2 + jc) * 128:(u * 2 + jc + 1) * 128],
                                v=slot[:, 1024 + jc * 512 + u * 128: 1024 + jc * 512 + (u + 1) * 128],
                                st=(jc == 0), sp=(jc == 1):
                                e.matmul(o, l, v, start=st, stop=sp),
                                sem="PE" if (u == 3 and jc == 1) else None)
            p.set_key(("A2slot", par), "PE", pe_c)
            p.set_key(("A2kdx", par), "PE", pe_c)
            p.wait("vector", "PE", pe_c)
            dvs = None
            for u in range(4):
                h = 4 * g + u
                p.wait_key("vector", ("Schain", h))
                dvs = p.op("vector", lambda e, o=S[:, h * 128:(h + 1) * 128],
                           s=bdv[:, h:h + 1], c=cb[:, u * 128:(u + 1) * 128]:
                           e.scalar_tensor_tensor(o, o, s, c, op0=ALU.mult, op1=ALU.add),
                           sem="DV")
                p.set_key(("Schain", h), "DV", dvs)
            p.set_key(("A2cb", par), "DV", dvs)

    p.wait("sync", "DV", p.cnt["DV"])
    src = S[:, :].rearrange("d (h e) -> d h e", h=H)
    dst = t["cc_in"].ap().rearrange("(h d) e -> d h e", h=H)
    p.dma("sync", dst, src, "ST")
    ccflag = p.barrier()
    p.wait("gpsimd", "FG", ccflag)
    p.op("gpsimd", lambda e: e.collective_compute(
        "AllGather", ALU.bypass, replica_groups=[[0, 1, 2, 3], [4, 5, 6, 7]],
        ins=[t["cc_in"].ap().opt()], outs=[t["cc_out"].ap().opt()]), sem="CC")
    p.set_key("A2_done_PE", "PE", p.cnt["PE"])
    p.set_key("A2_stores", "FG", ccflag)



def emit_A3(p, t):
    ext, PX, STG, banks = t["ext"], t["PX"], t["STG"], t["banks"]
    load, free = p._slab_load, p._slab_free
    for par in range(2):
        p.seed_key(("A3stg", par), f"SS{par}")
    for pan in range(16):
        src = ext["wg"][:, pan * NFI * 256:(pan + 1) * NFI * 256]
        slot, lsem, lcnt = load(src, NFI * 256)
        p.wait("tensor", lsem, lcnt)
        for tc in range(NTC):
            b = 4 + tc % 4
            p.wait_key("tensor", ("bank", b))
            for c in range(NFI):
                p.op("tensor", lambda e, o=banks[b][:, 0:256],
                     l=PX[:, c * T + tc * 128: c * T + tc * 128 + 128],
                     r=slot[:, c * 256:(c + 1) * 256],
                     st=(c == 0), sp=(c == NFI - 1):
                     e.matmul(o, l, r, start=st, stop=sp),
                     sem="PE" if c == NFI - 1 else None)
            pe_c = p.cnt["PE"]
            stg = STG[:, 3072 + (tc % 2) * 256: 3072 + (tc % 2) * 256 + 256]
            p.wait("scalar", "PE", pe_c)
            p.wait_key("scalar", ("A3stg", tc % 2))
            sc_c = p.op("scalar", lambda e, o=stg, i=banks[b][:, 0:256]:
                        e.activation(o, i, AF.Sigmoid), sem="SC")
            p.set_key(("bank", b), "SC", sc_c)
            p.wait("sync", "SC", sc_c)
            st_c = p.dma("sync", t["scr"]["g_scr"][tc, :, pan * 256:(pan + 1) * 256], stg, f"SS{tc % 2}")
            p.set_key(("A3stg", tc % 2), f"SS{tc % 2}", st_c)
        free()
    p.set_key("A3_stores", "FG", p.barrier())



def emit_sstart(p, t):
    CST, S, ccw = t["CST"], t["S"], t["ccw"]
    cc_out = t["cc_out"].ap()
    p.wait("scalar", "CC", p.cnt["CC"])
    p.wait_key("scalar", "A2_done_PE")
    aggv = CST[:, :].bitcast(F32)
    for h in range(H):
        par = h % 2
        agg = aggv[:, par * 512:par * 512 + 512]
        sem = "GL0" if par == 0 else "GL1"
        p.wait_key("scalar", ("aggslot", par))
        lc = None
        for r in range(4):
            lc = p.dma("scalar", agg[:, r * 128:(r + 1) * 128],
                       cc_out[r * H * 128 + h * 128: r * H * 128 + (h + 1) * 128, :], sem)
        p.wait("vector", sem, lc)
        p.wait_key("vector", ("Schain", h))
        dv = None
        for r in range(4):
            if dv is not None:
                p.wait("vector", "DV", dv)
            if r == 0:
                dv = p.op("vector", lambda e, o=S[:, h * 128:(h + 1) * 128],
                          i=agg[:, 0:128], s=ccw[:, h * 4:h * 4 + 1]:
                          e.tensor_scalar(o, i, s, None, op0=ALU.mult), sem="DV")
            else:
                dv = p.op("vector", lambda e, o=S[:, h * 128:(h + 1) * 128],
                          i=agg[:, r * 128:(r + 1) * 128],
                          s=ccw[:, h * 4 + r:h * 4 + r + 1],
                          m=S[:, h * 128:(h + 1) * 128]:
                          e.scalar_tensor_tensor(o, i, s, m, op0=ALU.mult, op1=ALU.add),
                          sem="DV")
        p.set_key(("aggslot", par), "DV", dv)
        p.set_key(("Schain", h), "DV", dv)
    if "S_dump" in t["tapped"]:
        p.wait("sync", "DV", p.cnt["DV"])
        p.dma("sync", t["tapped"]["S_dump"], t["S"][:, :], "ST")



def emit_B(p, t):
    banks, STG, PX, S = t["banks"], t["STG"], t["PX"], t["S"]
    bdv, ident, lnt = t["bdv"], t["ident"], t["lnt"]
    qscr, kscr, vscr, kdscr = (t["scr"][n] for n in ["q_scr", "k_scr", "v_scr", "kd_scr"])
    gscr, zscr, mscr = t["scr"]["g_scr"], t["scr"]["z_scr"], t["scr"]["qdm_scr"]

    attn = PX[:, 0:8192]
    gate_slot = [PX[:, 8192:16384], PX[:, 16384:24576]]
    zbf = PX[:, 24576:32768]
    CST = t["CST"]
    gslots = [PX[:, 32768 + i * 4096: 32768 + (i + 1) * 4096] for i in range(4)]
    mbuf = [PX[:, 49152:52224], PX[:, 52224:55296]]
    sdbuf = [STG[:, 0:2048], STG[:, 2048:4096],
             PX[:, 55296:57344], PX[:, 57344:59392]]
    qdbuf = [PX[:, 59392 + i * 1024: 59392 + (i + 1) * 1024] for i in range(4)]
    sbfbuf = [PX[:, 63488 + i * 512: 63488 + (i + 1) * 512] for i in range(4)]
    zstg = [CST[:, 0:512], CST[:, 512:1024]]

    p.wait_key("sync", "A3_stores")
    p.wait_key("sync", "A2_stores")
    p.seed_key("BattnSC", "PE")
    p.seed_key("BattnDV", "PE")
    for i in range(2):
        p.seed_key(("Bzb", i), "PE")
        p.seed_key(("Bgate", i), "PE")
        p.seed_key(("Bslot", i), "PE")
        p.seed_key(("Bslot", 2 + i), "PE")
        p.seed_key(("Bsdbuf", i), "PE")
        p.seed_key(("Bsdbuf", 2 + i), "PE")
        p.seed_key(("Bqdbuf", i), "PE")
        p.seed_key(("Bqdbuf", 2 + i), "PE")
        p.seed_key(("Bsbfbuf", i), "PE")
        p.seed_key(("Bsbfbuf", 2 + i), "PE")
        p.seed_key(("Bmask", i), "PE")
        p.seed_key(("Bk23", i), "SC")
        p.seed_key(("Bk01", i), "DV")
        p.seed_key(("Bzstg", i), f"SS{i}")
    for u in range(4):
        p.seed_key(("BbankS", u), "SC")
    p.seed_key("Baggdone", "DV")

    def stage1(b, g):
        par = (b * 8 + g) % 2
        spar = (b * 8 + g) % 4
        qsem = ["QL0", "QL1", "XL0", "XL1"][spar]
        msem = "ML0" if par == 0 else "ML1"
        slot = gslots[spar]
        mb = mbuf[par]
        p.wait_key("sync", ("Bslot", spar))
        if b == 0 and g < 4:
            kk = p.keys.get("Baggdone")
            p.wait("sync", kk[0], kk[1])
        p.dma("sync", slot[:, 0:1024],
              qscr[b, 4 * g:4 * g + 4].rearrange("h q t -> q h t"), qsem)
        p.dma("sync", slot[:, 1024:2048],
              kscr[b, 4 * g:4 * g + 4].rearrange("h q t -> q h t"), qsem)
        p.dma("sync", slot[:, 2048:3072],
              kdscr[b, 4 * g:4 * g + 4].rearrange("h q t -> q h t"), qsem)
        p.dma("sync", slot[:, 3072:3584],
              vscr[2 * b, 2 * g:2 * g + 2].rearrange("n q t -> q n t"), qsem)
        lcnt = p.dma("sync", slot[:, 3584:4096],
                     vscr[2 * b + 1, 2 * g:2 * g + 2].rearrange("n q t -> q n t"), qsem)
        p.wait_key("sync", ("Bmask", par))
        mcnt = p.dma("sync", mb[:, :],
                     mscr[:, (4 * g) * 768:(4 * g + 4) * 768], msem)
        p.wait("tensor", qsem, lcnt)
        for u in range(4):
            p.wait_key("tensor", ("BbankS", u))
        pe_s = None
        for u in range(4):
            for jc in range(2):
                pe_s = p.op("tensor", lambda e,
                            o=banks[4 + u][:, jc * 256:(jc + 1) * 256],
                            l=slot[:, 1024 + u * 256 + jc * 128: 1024 + u * 256 + (jc + 1) * 128],
                            r=slot[:, u * 256:(u + 1) * 256]:
                            e.matmul(o, l, r, start=True, stop=True),
                            sem="PE" if (u == 3 and jc == 1) else None)
        sd = sdbuf[spar]
        p.wait("vector", "PE", pe_s)
        p.wait("vector", msem, mcnt)
        p.wait_key("vector", ("Bsdbuf", spar))
        dv_sd = {}
        for u in range(4):
            dv_sd[u] = p.op("vector", lambda e, o=sd[:, u * 512:(u + 1) * 512],
                            i=banks[4 + u][:, 0:512],
                            m=mb[:, u * 768: u * 768 + 512]:
                            e.tensor_tensor(o, i, m, op=ALU.mult), sem="DV")
            p.set_key(("BbankS", u), "DV", dv_sd[u])
        qd = qdbuf[spar]
        p.wait_key("vector", ("Bqdbuf", spar))
        dv_qd = None
        for u in range(4):
            dv_qd = p.op("vector", lambda e, o=qd[:, u * 256:(u + 1) * 256],
                         i=slot[:, u * 256:(u + 1) * 256],
                         m=mb[:, u * 768 + 512: (u + 1) * 768]:
                         e.tensor_tensor(o, i, m, op=ALU.mult),
                         sem="DV" if u == 3 else None)
        p.set_key(("Bmask", par), "DV", dv_qd)
        sbf = sbfbuf[spar]
        for u in range(4):
            kk = p.keys.get(("Schain", 4 * g + u))
            if kk is not None:
                p.wait("scalar", kk[0], kk[1])
        p.wait_key("scalar", ("Bsbfbuf", spar))
        sc_b = p.op("scalar", lambda e, o=sbf[:, :],
                    i=S[:, (4 * g) * 128:(4 * g + 4) * 128]: e.copy(o, i), sem="SC")
        return dict(b=b, g=g, par=par, spar=spar, slot=slot, sd=sd, qd=qd,
                    sbf=sbf, dv_sd=dv_sd[3], dv_qd=dv_qd, sc_b=sc_b)

    def stage2(r):
        b, g, par, slot = r["b"], r["g"], r["par"], r["slot"]
        sd, qd, sbf = r["sd"], r["qd"], r["sbf"]
        gpar = (b * 8 + g) % 2
        p.wait("tensor", "DV", r["dv_qd"])
        p.wait("tensor", "DV", r["dv_sd"])
        p.wait("tensor", "SC", r["sc_b"])
        p.wait_key("tensor", ("Bk23", 0))
        p.wait_key("tensor", ("Bk23", 1))
        pe_P = None
        for u in range(4):
            pb = banks[2 + u // 2]
            base = (u % 2) * 256
            for ic in range(2):
                o = pb[:, base + ic * 128: base + (ic + 1) * 128]
                p.op("tensor", lambda e, o=o,
                     l=sd[:, u * 512 + ic * 128: u * 512 + ic * 128 + 128],
                     v=slot[:, 3072 + u * 128: 3072 + (u + 1) * 128]:
                     e.matmul(o, l, v, start=True, stop=False))
                p.op("tensor", lambda e, o=o,
                     l=sd[:, u * 512 + 256 + ic * 128: u * 512 + 256 + ic * 128 + 128],
                     v=slot[:, 3584 + u * 128: 3584 + (u + 1) * 128]:
                     e.matmul(o, l, v, start=False, stop=False))
                pe_P = p.op("tensor", lambda e, o=o,
                            l=qd[:, u * 256 + ic * 128: u * 256 + (ic + 1) * 128],
                            v=sbf[:, u * 128:(u + 1) * 128]:
                            e.matmul(o, l, v, start=False, stop=True),
                            sem="PE" if (u == 3 and ic == 1) else None)
        p.set_key(("Bsdbuf", r["spar"]), "PE", pe_P)
        p.set_key(("Bqdbuf", r["spar"]), "PE", pe_P)
        p.set_key(("Bsbfbuf", r["spar"]), "PE", pe_P)
        cb = banks[gpar]
        p.wait_key("tensor", ("Bk01", gpar))
        pe_C = None
        for u in range(4):
            for jc in range(2):
                pe_C = p.op("tensor", lambda e, o=cb[:, u * 128:(u + 1) * 128],
                            l=slot[:, 2048 + u * 256 + jc * 128: 2048 + u * 256 + (jc + 1) * 128],
                            v=slot[:, 3072 + jc * 512 + u * 128: 3072 + jc * 512 + (u + 1) * 128],
                            st=(jc == 0), sp=(jc == 1):
                            e.matmul(o, l, v, start=st, stop=sp),
                            sem="PE" if (u == 3 and jc == 1) else None)
        p.set_key(("Bslot", r["spar"]), "PE", pe_C)
        p.wait("scalar", "PE", pe_P)
        p.wait_key("scalar", "BattnSC")
        p.wait_key("scalar", "BattnDV")
        sc = None
        for u in range(4):
            pb = banks[2 + u // 2]
            base = (u % 2) * 256
            src2 = pb[:, base:base + 256].rearrange("q (ic f) -> q ic f", ic=2)
            dst = attn.rearrange("q (ic w) -> q ic w", ic=2)[:, :, (4 * g + u) * 128:(4 * g + u + 1) * 128]
            sc = p.op("scalar", lambda e, o=dst, i=src2: e.copy(o, i), sem="SC")
            if u % 2 == 1:
                p.set_key(("Bk23", u // 2), "SC", sc)
        p.set_key(("Battnw", b), "SC", sc)
        p.wait("vector", "PE", pe_C)
        p.wait("vector", "SC", r["sc_b"])
        dvs = None
        for u in range(4):
            h = 4 * g + u
            kk = p.keys.get(("Schain", h))
            p.wait("vector", kk[0], kk[1])
            dvs = p.op("vector", lambda e, o=S[:, h * 128:(h + 1) * 128],
                       s=bdv[:, h:h + 1], c=cb[:, u * 128:(u + 1) * 128]:
                       e.scalar_tensor_tensor(o, o, s, c, op0=ALU.mult, op1=ALU.add),
                       sem="DV")
            p.set_key(("Schain", h), "DV", dvs)
        p.set_key(("Bk01", gpar), "DV", dvs)

    pend = []
    for b in range(NB):
        gsem = "GL0" if b % 2 == 0 else "GL1"
        gslot = gate_slot[b % 2]
        p.wait_key("sync", ("Bgate", b % 2))
        if b < 2:
            kk = p.keys.get("Baggdone")
            p.wait("sync", kk[0], kk[1])
        p.dma("sync", gslot[:, 0:4096], gscr[2 * b, :, :], gsem)
        g_c = p.dma("sync", gslot[:, 4096:8192], gscr[2 * b + 1, :, :], gsem)
        p.set_key(("Bgater", b % 2), gsem, g_c)

        for g in range(8):
            rec = stage1(b, g)
            pend.append(rec)
            if len(pend) > 2:
                stage2(pend.pop(0))
        while pend:
            stage2(pend.pop(0))

        p.wait_key("vector", ("Bgater", b % 2))
        lastw = p.keys.get(("Battnw", b))
        for ic in range(2):
            att = attn[:, ic * 4096:(ic + 1) * 4096]
            zb = zbf[:, ic * 4096:(ic + 1) * 4096]
            p.wait("scalar", lastw[0], lastw[1])
            p.wait_key("scalar", ("Bzb", ic))
            ssq = lnt[:, 32 + ic:33 + ic]
            sc1 = p.op("scalar", lambda e, o=zb, i=att, a=ssq:
                       e.activation(o, i, AF.Square, accum_out=a), sem="SC")
            p.wait("scalar", "SC", sc1)
            lnv = lnt[:, 34 + ic:35 + ic]
            sc2 = p.op("scalar", lambda e, o=lnv, i=ssq:
                       e.activation(o, i, AF.Ln, scale=1.0 / D, bias=t["epsc"][:, 0:1]), sem="SC")
            p.wait("scalar", "SC", sc2)
            rstd = lnt[:, 36 + ic:37 + ic]
            sc3 = p.op("scalar", lambda e, o=rstd, i=lnv:
                       e.activation(o, i, AF.Exp, scale=-0.5), sem="SC")
            gat = gate_slot[b % 2][:, ic * 4096:(ic + 1) * 4096]
            p.wait("vector", "SC", sc3)
            dv1 = p.op("vector", lambda e, o=zb, i=att, s=rstd, g2=gat:
                       e.scalar_tensor_tensor(o, i, s, g2, op0=ALU.mult, op1=ALU.mult),
                       sem="DV")
            if ic == 1:
                p.set_key(("Bgate", b % 2), "DV", dv1)
                p.set_key("BattnDV", "DV", dv1)
            tc = 2 * b + ic
            for grp in range(8):
                tbk = banks[grp % 2].ap().bitcast(BF16)
                p.wait("tensor", "DV", dv1)
                p.wait_key("tensor", ("Bk01", grp % 2))
                for j in range(4):
                    c = grp * 4 + j
                    p.op("tensor", lambda e, o=tbk[:, j * 128:(j + 1) * 128],
                         i=zb[:, c * 128:(c + 1) * 128], idt=ident[:, :]:
                         e.transpose(o, i, idt),
                         sem="PE" if j == 3 else None)
                pe_t = p.cnt["PE"]
                stg = zstg[grp % 2]
                p.wait("vector", "PE", pe_t)
                p.wait_key("vector", ("Bzstg", grp % 2))
                dv_e = p.op("vector", lambda e, o=stg, i=tbk[:, 0:512]:
                            e.tensor_copy(o, i), sem="DV")
                p.set_key(("Bk01", grp % 2), "DV", dv_e)
                p.wait("sync", "DV", dv_e)
                st_c = p.dma("sync", zscr[:, tc * 4096 + grp * 512: tc * 4096 + (grp + 1) * 512],
                             stg, f"SS{grp % 2}")
                p.set_key(("Bzstg", grp % 2), f"SS{grp % 2}", st_c)
            p.set_key(("Bzb", ic), "PE", p.cnt["PE"])
        p.set_key("BattnSC", "SC", p.cnt["SC"])

    if "SB_dump" in t["tapped"]:
        p.wait("sync", "DV", p.cnt["DV"])
        p.dma("sync", t["tapped"]["SB_dump"], S[:, :], "ST")
    p.set_key("B_stores", "FG", p.barrier())
    p.set_key("B_PE_done", "PE", p.cnt["PE"])



def emit_C(p, t):
    ext, PX, STG, banks = t["ext"], t["PX"], t["STG"], t["banks"]
    load, free = p._slab_load, p._slab_free
    zscr, plscr = t["scr"]["z_scr"], t["scr"]["pl1_scr"]

    p.wait_key("gpsimd", "B_stores")
    p.wait_key("gpsimd", "B_PE_done")
    p.wait("gpsimd", "DV", p.cnt["DV"])
    p.wait("gpsimd", "SC", p.cnt["SC"])
    zl = None
    for i in range(8):
        w = NTC * D // 8
        zl = p.dma("gpsimd", PX[:, i * w:(i + 1) * w], zscr[:, i * w:(i + 1) * w], "LD")
    p.set_key("z_resident", "LD", zl)

    xres = STG[:, 0:2048].bitcast(F32)
    stg_o = STG[:, 2048:4096].bitcast(F32)
    for par in range(2):
        p.seed_key(("Cxs", par), "PE")
        p.seed_key(("Cstg", par), "PE")
        p.seed_key(("Css", par), f"SS{2 + par}")

    nwc = t["nwc"]
    for s in range(16):
        src = ext["wo"][:, s * NFI * 256:(s + 1) * NFI * 256]
        slot, lsem, lcnt = load(src, NFI * 256)
        p.wait("vector", lsem, lcnt)
        p.wait_key("vector", ("Cnw", 0))
        dvw = None
        for c in range(NFI):
            dvw = p.op("vector", lambda e, o=slot[:, c * 256:(c + 1) * 256],
                       i=slot[:, c * 256:(c + 1) * 256], sc=nwc[:, c:c + 1]:
                       e.tensor_scalar(o, i, sc, None, op0=ALU.mult),
                       sem="DV" if c == NFI - 1 else None)
        p.wait("tensor", "DV", dvw)
        if s == 0:
            p.wait_key("tensor", "z_resident")
        for tc in range(NTC):
            par = tc % 2
            xsem = "XL0" if par == 0 else "XL1"
            p.wait_key("scalar", ("Cxs", par))
            xc = p.dma("scalar", xres[:, par * 256:(par + 1) * 256],
                       ext["x_tm"][tc * 128:(tc + 1) * 128, s * 256:(s + 1) * 256], xsem)
            b = tc % 8
            p.wait_key("tensor", ("bank", b))
            for c in range(NFI):
                p.op("tensor", lambda e, o=banks[b][:, 0:256],
                     l=PX[:, tc * 4096 + c * 128: tc * 4096 + (c + 1) * 128],
                     r=slot[:, c * 256:(c + 1) * 256],
                     st=(c == 0), sp=(c == NFI - 1):
                     e.matmul(o, l, r, start=st, stop=sp),
                     sem="PE" if c == NFI - 1 else None)
            pe_c = p.cnt["PE"]
            p.wait("vector", "PE", pe_c)
            p.wait("vector", xsem, xc)
            p.wait_key("vector", ("Cstg", par))
            dv_c = p.op("vector", lambda e, o=stg_o[:, par * 256:(par + 1) * 256],
                        i=banks[b][:, 0:256], x=xres[:, par * 256:(par + 1) * 256]:
                        e.tensor_tensor(o, i, x, op=ALU.add), sem="DV")
            p.set_key(("bank", b), "DV", dv_c)
            p.set_key(("Cxs", par), "DV", dv_c)
            p.wait("sync", "DV", dv_c)
            p.wait_key("sync", ("Css", par))
            st_c = p.dma("sync", plscr[tc, :, s * 256:(s + 1) * 256],
                         stg_o[:, par * 256:(par + 1) * 256], f"SS{2 + par}")
            p.set_key(("Cstg", par), f"SS{2 + par}", st_c)
        free()
        p.set_key(("Cnw", 0), "PE", p.cnt["PE"])
    p.set_key("C_stores", "FG", p.barrier())
    p.set_key("C_PE_done", "PE", p.cnt["PE"])



def emit_LN1(p, t):
    PX, STG, banks, ident, lnt = t["PX"], t["STG"], t["banks"], t["ident"], t["lnt"]
    plscr, x1scr, x1fm = t["scr"]["pl1_scr"], t["scr"]["x1_scr"], t["scr"]["x1fm_scr"]
    row = [PX[:, 0:8192].bitcast(F32), PX[:, 8192:16384].bitcast(F32)]
    x1bf = [PX[:, 16384:20480], PX[:, 20480:24576]]

    cs = p.keys.get("C_stores")
    p.seed_key(("L1row", 0), "PE")
    p.seed_key(("L1row", 1), "PE")
    p.seed_key(("L1x1", 0), "PE")
    p.seed_key(("L1x1", 1), "PE")
    for i in range(2):
        p.seed_key(("L1stg", i), f"SS{i}")
        p.seed_key(("L1xrs", i), f"SS{2 + i}")

    for tc in range(NTC):
        par = tc % 2
        xsem = "XL0" if par == 0 else "XL1"
        p.wait_key("scalar", ("L1row", par))
        p.wait("scalar", cs[0], cs[1])
        rc_ = p.dma("scalar", row[par][:, :], plscr[tc, :, :], xsem)
        p.wait("vector", xsem, rc_)
        for i in range(8):
            p.op("vector", lambda e, o=lnt[:, 6 * i:6 * (i + 1)],
                 i2=row[par][:, i * 512:(i + 1) * 512]:
                 e.bn_stats(o, i2), sem="DV" if i == 7 else None)
        p.wait("vector", "DV", p.cnt["DV"])
        dva = p.op("vector", lambda e, o=lnt[:, 48:50], i=lnt[:, 0:48]:
                   e.bn_aggr(o, i), sem="DV")
        p.wait("scalar", "DV", dva)
        s1 = p.op("scalar", lambda e, o=lnt[:, 50:51], i=lnt[:, 49:50]:
                  e.activation(o, i, AF.Ln, bias=t["epsc"][:, 0:1]), sem="SC")
        p.wait("scalar", "SC", s1)
        s2 = p.op("scalar", lambda e, o=lnt[:, 51:52], i=lnt[:, 50:51]:
                  e.activation(o, i, AF.Exp, scale=-0.5), sem="SC")
        p.wait("vector", "SC", s2)
        dvn = p.op("vector", lambda e, o=row[par][:, :], i=row[par][:, :],
                   m=lnt[:, 48:49], r=lnt[:, 51:52]:
                   e.tensor_scalar(o, i, m, r, op0=ALU.subtract, op1=ALU.mult), sem="DV")
        p.wait("sync", "DV", dvn)
        p.wait_key("sync", ("L1xrs", par))
        st1 = p.dma("sync", x1scr[tc, :, :], row[par][:, :], f"SS{2 + par}")
        p.set_key(("L1xrs", par), f"SS{2 + par}", st1)
        p.wait("vector", "DV", dvn)
        p.wait_key("vector", ("L1x1", par))
        dvb = p.op("vector", lambda e, o=x1bf[par][:, :], i=row[par][:, :]:
                   e.tensor_copy(o, i), sem="DV")
        p.set_key(("L1row", par), "DV", dvb)
        for grp in range(8):
            tbk = banks[grp % 2].ap().bitcast(BF16)
            p.wait("tensor", "DV", dvb)
            p.wait_key("tensor", ("L1tb", grp % 2))
            if tc == 0 and grp < 2:
                p.wait_key("tensor", ("Bk01", grp % 2))
            for j in range(4):
                c = grp * 4 + j
                p.op("tensor", lambda e, o=tbk[:, j * 128:(j + 1) * 128],
                     i=x1bf[par][:, c * 128:(c + 1) * 128], idt=ident[:, :]:
                     e.transpose(o, i, idt),
                     sem="PE" if j == 3 else None)
            pe_t = p.cnt["PE"]
            stg = STG[:, (grp % 2) * 512:(grp % 2 + 1) * 512]
            p.wait("scalar", "PE", pe_t)
            p.wait_key("scalar", ("L1stg", grp % 2))
            sc_e = p.op("scalar", lambda e, o=stg, i=tbk[:, 0:512]: e.copy(o, i), sem="SC")
            p.set_key(("L1tb", grp % 2), "SC", sc_e)
            p.wait("sync", "SC", sc_e)
            st_c = p.dma("sync", x1fm[:, tc * 4096 + grp * 512: tc * 4096 + (grp + 1) * 512],
                         stg, f"SS{grp % 2}")
            p.set_key(("L1stg", grp % 2), f"SS{grp % 2}", st_c)
        p.set_key(("L1x1", par), "PE", p.cnt["PE"])
        if tc % 4 == 3:
            p.set_key(("L1tc", tc), "FG", p.barrier())
    p.set_key("L1_stores", "FG", p.cnt["FG"])



def emit_D1(p, t):
    ext, PX, banks = t["ext"], t["PX"], t["banks"]
    x1fm = t["scr"]["x1fm_scr"]
    load, free = p._slab_load, p._slab_free

    xsl = [PX[:, 28672:45056], PX[:, 0:16384]]
    Pfm = PX[:, 49152:65536]
    p.seed_key("D1_pfm_gate", "PE")
    for par in range(2):
        p.seed_key(("D1x", par), "PE")

    for k in range(4):
        par = k % 2
        xsem = "GL0" if par == 0 else "GL1"
        p.wait_key("scalar", ("D1x", par))
        kk = p.keys.get(("L1tc", 4 * k + 3))
        p.wait("scalar", kk[0], kk[1])
        if k == 1:
            p.wait_key("scalar", "L1_stores")
        xc = p.dma("scalar", xsl[par][:, :], x1fm[:, k * 16384:(k + 1) * 16384], xsem)
        for rcn in range(8):
            src2 = ext["u1"][:, rcn * NFI * 128:(rcn + 1) * NFI * 128]
            slot, lsem, lcnt = load(src2, NFI * 128)
            b = rcn % 8
            p.wait_key("tensor", ("bank", b))
            p.wait("tensor", xsem, xc)
            p.wait("tensor", lsem, lcnt)
            for c in range(NFI):
                rhs = xsl[par].rearrange("q (tc cc j) -> q tc cc j", tc=4, cc=NFI)[:, :, c, :]
                p.op("tensor", lambda e, o=banks[b][:, :],
                     l=slot[:, c * 128:(c + 1) * 128], r=rhs,
                     st=(c == 0), sp=(c == NFI - 1):
                     e.matmul(o, l, r, start=st, stop=sp),
                     sem="PE" if c == NFI - 1 else None)
            pe_c = p.cnt["PE"]
            free()
            p.wait("vector", "PE", pe_c)
            if k == 0 and rcn == 0:
                p.wait_key("vector", "D1_pfm_gate")
            dv_c = p.op("vector", lambda e,
                        o=Pfm[:, rcn * T + k * 512: rcn * T + (k + 1) * 512],
                        i=banks[b][:, :]: e.tensor_copy(o, i), sem="DV")
            p.set_key(("bank", b), "DV", dv_c)
            if rcn == 7:
                p.set_key(("D1x", par), "DV", dv_c)
    p.set_key("Pfm_ready", "DV", p.cnt["DV"])



def emit_D23(p, t):
    ext, PX, STG, banks = t["ext"], t["PX"], t["STG"], t["banks"]
    b1c = t["b1c"]
    load, free = p._slab_load, p._slab_free
    Pfm = PX[:, 49152:65536]
    act = [PX[:, i * 2048:(i + 1) * 2048] for i in range(8)]
    Z = PX[:, 16384:49152].bitcast(F32)
    silug = STG[:, 0:2048]

    pw = p.keys.get("Pfm_ready")
    for tt in range(4):
        p.seed_key(("D2sg", tt), "ST")
    for i in range(8):
        p.seed_key(("D2act", i), "PE")
    p.seed_key("D3z_gate", "PE")

    first_group = True
    idx = 0

    def h_chunk(fcpos, is_gate, act_slot):
        nonlocal idx
        src = ext["v1"][:, fcpos * 8 * 128:(fcpos + 1) * 8 * 128]
        slot, lsem, lcnt = load(src, 8 * 128)
        p.wait("tensor", lsem, lcnt)
        if idx == 0:
            p.wait("tensor", pw[0], pw[1])
        idx += 1
        for tt in range(4):
            b = tt % 4
            p.wait_key("tensor", ("bank", b))
            for rcn in range(8):
                p.op("tensor", lambda e, o=banks[b][:, :],
                     l=slot[:, rcn * 128:(rcn + 1) * 128],
                     r=Pfm[:, rcn * T + tt * 512: rcn * T + (tt + 1) * 512],
                     st=(rcn == 0), sp=(rcn == 7):
                     e.matmul(o, l, r, start=st, stop=sp),
                     sem="PE" if rcn == 7 else None)
            pe_c = p.cnt["PE"]
            if is_gate:
                p.wait("scalar", "PE", pe_c)
                p.wait_key("scalar", ("D2sg", tt))
                sc_c = p.op("scalar", lambda e, o=silug[:, tt * 512:(tt + 1) * 512],
                            i=banks[b][:, :], bb=b1c[:, fcpos:fcpos + 1]:
                            e.activation(o, i, AF.Silu, bias=bb), sem="SC")
                p.set_key(("bank", b), "SC", sc_c)
                p.set_key(("D2sgw", tt), "SC", sc_c)
            else:
                p.wait("vector", "PE", pe_c)
                kk = p.keys.get(("D2sgw", tt))
                p.wait("vector", kk[0], kk[1])
                p.wait_key("vector", ("D2act", act_slot))
                dv_c = p.op("vector", lambda e,
                            o=act[act_slot][:, tt * 512:(tt + 1) * 512],
                            i=banks[b][:, :], bb=b1c[:, fcpos:fcpos + 1],
                            g=silug[:, tt * 512:(tt + 1) * 512]:
                            e.scalar_tensor_tensor(o, i, bb, g, op0=ALU.add, op1=ALU.mult),
                            sem="DV")
                p.set_key(("bank", b), "DV", dv_c)
                p.set_key(("D2sg", tt), "DV", dv_c)
                if tt == 3:
                    p.set_key(("actready", act_slot), "DV", dv_c)
        free()

    for g0 in range(0, GCH, 4):
        gsz = min(4, GCH - g0)
        for j in range(gsz):
            fc = g0 + j
            h_chunk(2 * fc, True, fc % 8)
            h_chunk(2 * fc + 1, False, fc % 8)
        src = ext["u2"][:, g0 * RANK:(g0 + gsz) * RANK]
        slot, lsem, lcnt = load(src, gsz * RANK)
        p.wait("tensor", lsem, lcnt)
        kk = p.keys.get(("actready", (g0 + gsz - 1) % 8))
        p.wait("tensor", kk[0], kk[1])
        if first_group:
            p.wait_key("vector", "D3z_gate")
        for rcn in range(8):
            for tt in range(4):
                b = 4 + (rcn * 4 + tt) % 4
                p.wait_key("tensor", ("bank", b))
                for j in range(gsz):
                    p.op("tensor", lambda e, o=banks[b][:, :],
                         l=slot[:, j * RANK + rcn * 128: j * RANK + (rcn + 1) * 128],
                         r=act[(g0 + j) % 8][:, tt * 512:(tt + 1) * 512],
                         st=(j == 0), sp=(j == gsz - 1):
                         e.matmul(o, l, r, start=st, stop=sp),
                         sem="PE" if j == gsz - 1 else None)
                pe_c = p.cnt["PE"]
                p.wait("vector", "PE", pe_c)
                zsl = Z[:, rcn * T + tt * 512: rcn * T + (tt + 1) * 512]
                if first_group:
                    dv_c = p.op("vector", lambda e, o=zsl, i=banks[b][:, :]:
                                e.tensor_copy(o, i), sem="DV")
                else:
                    p.wait_key("vector", ("Zsl", (rcn, tt)))
                    dv_c = p.op("vector", lambda e, o=zsl, i=banks[b][:, :], z=zsl:
                                e.tensor_tensor(o, i, z, op=ALU.add), sem="DV")
                p.set_key(("bank", b), "DV", dv_c)
                p.set_key(("Zsl", (rcn, tt)), "DV", dv_c)
        grp_pe = p.cnt["PE"]
        for j in range(gsz):
            p.set_key(("D2act", (g0 + j) % 8), "PE", grp_pe)
        first_group = False
        free()
    p.set_key("Z_ready", "DV", p.cnt["DV"])
    p.set_key("Pfm_dead", "PE", p.cnt["PE"])



def emit_D4(p, t):
    ext, PX, STG, banks = t["ext"], t["PX"], t["STG"], t["banks"]
    ones, lnt = t["ones"], t["lnt"]
    x1scr, pl2scr, OUT = t["scr"]["x1_scr"], t["scr"]["pl2_scr"], t["OUT"]
    load, free = p._slab_load, p._slab_free
    Z = PX[:, 16384:49152].bitcast(F32)
    Zbf = PX[:, 0:16384]
    x1res = STG[:, 0:2048].bitcast(F32)
    stg_o = STG[:, 2048:4096].bitcast(F32)

    zr = p.keys.get("Z_ready")
    pfd = p.keys.get("Pfm_dead")
    p.wait("vector", zr[0], zr[1])
    p.wait("vector", pfd[0], pfd[1])
    dvz = None
    for rcn in range(8):
        dvz = p.op("vector", lambda e, o=Zbf[:, rcn * T:(rcn + 1) * T],
                   i=Z[:, rcn * T:(rcn + 1) * T]: e.tensor_copy(o, i), sem="DV")
    p.set_key("Zbf_ready", "DV", dvz)

    for par in range(2):
        p.seed_key(("D4x", par), "PE")
        p.seed_key(("D4stg", par), f"SS{par}")

    lw = p.keys.get("L1_stores")
    b2t = t["b2"]
    for s in range(8):
        src2 = ext["v2"][:, s * 4096:(s + 1) * 4096]
        slot, lsem, lcnt = load(src2, 4096)
        bpar = s % 2
        bsem = "B20" if bpar == 0 else "B21"
        p.wait_key("gpsimd", ("D4b2", bpar))
        b2c = p.dma("gpsimd", b2t[0:1, bpar * 512:(bpar + 1) * 512],
                    ext["b2"][0:1, s * 512:(s + 1) * 512], bsem)
        p.wait("tensor", lsem, lcnt)
        p.wait("tensor", bsem, b2c)
        if s == 0:
            p.wait_key("tensor", "Zbf_ready")
        for tc in range(NTC):
            par = tc % 2
            xsem = "XL0" if par == 0 else "XL1"
            p.wait_key("scalar", ("D4x", par))
            if s == 0 and tc < 2:
                p.wait("scalar", lw[0], lw[1])
            xc = p.dma("scalar", x1res[:, par * 512:(par + 1) * 512],
                       x1scr[tc, :, s * 512:(s + 1) * 512], xsem)
            b = tc % 8
            p.wait_key("tensor", ("bank", b))
            for rcn in range(8):
                p.op("tensor", lambda e, o=banks[b][:, :],
                     l=Zbf[:, rcn * T + tc * 128: rcn * T + tc * 128 + 128],
                     r=slot[:, rcn * 512:(rcn + 1) * 512],
                     st=(rcn == 0), sp=False:
                     e.matmul(o, l, r, start=st, stop=sp))
            p.op("tensor", lambda e, o=banks[b][:, :],
                 l=ones[0:1, 0:128], r=b2t[0:1, bpar * 512:(bpar + 1) * 512]:
                 e.matmul(o, l, r, start=False, stop=True), sem="PE")
            pe_c = p.cnt["PE"]
            p.wait("vector", "PE", pe_c)
            p.wait("vector", xsem, xc)
            p.wait_key("vector", ("D4stg", par))
            dv_c = p.op("vector", lambda e, o=stg_o[:, par * 512:(par + 1) * 512],
                        i=banks[b][:, :], x=x1res[:, par * 512:(par + 1) * 512]:
                        e.tensor_tensor(o, i, x, op=ALU.add), sem="DV")
            p.set_key(("bank", b), "DV", dv_c)
            p.set_key(("D4x", par), "DV", dv_c)
            p.wait("sync", "DV", dv_c)
            st_c = p.dma("sync", pl2scr[tc, :, s * 512:(s + 1) * 512],
                         stg_o[:, par * 512:(par + 1) * 512], f"SS{par}")
            p.set_key(("D4stg", par), f"SS{par}", st_c)
        free()
        p.set_key(("D4b2", bpar), "PE", p.cnt["PE"])
    p.set_key("D4a_stores", "FG", p.barrier())

    row = [PX[:, 16384:24576].bitcast(F32), PX[:, 24576:32768].bitcast(F32)]
    d4s = p.keys.get("D4a_stores")
    p.seed_key(("L2row", 0), "DV")
    p.seed_key(("L2row", 1), "DV")
    for tc in range(NTC):
        par = tc % 2
        xsem = "XL0" if par == 0 else "XL1"
        p.wait_key("scalar", ("L2row", par))
        p.wait("scalar", d4s[0], d4s[1])
        rc_ = p.dma("scalar", row[par][:, :], pl2scr[tc, :, :], xsem)
        p.wait("vector", xsem, rc_)
        for i in range(8):
            p.op("vector", lambda e, o=lnt[:, 6 * i:6 * (i + 1)],
                 i2=row[par][:, i * 512:(i + 1) * 512]:
                 e.bn_stats(o, i2), sem="DV" if i == 7 else None)
        p.wait("vector", "DV", p.cnt["DV"])
        dva = p.op("vector", lambda e, o=lnt[:, 48:50], i=lnt[:, 0:48]:
                   e.bn_aggr(o, i), sem="DV")
        p.wait("scalar", "DV", dva)
        s1 = p.op("scalar", lambda e, o=lnt[:, 52:53], i=lnt[:, 49:50]:
                  e.activation(o, i, AF.Ln, bias=t["epsc"][:, 0:1]), sem="SC")
        p.wait("scalar", "SC", s1)
        s2 = p.op("scalar", lambda e, o=lnt[:, 53:54], i=lnt[:, 52:53]:
                  e.activation(o, i, AF.Exp, scale=-0.5), sem="SC")
        p.wait("vector", "SC", s2)
        dvn = p.op("vector", lambda e, o=row[par][:, :], i=row[par][:, :],
                   m=lnt[:, 48:49], r=lnt[:, 53:54]:
                   e.tensor_scalar(o, i, m, r, op0=ALU.subtract, op1=ALU.mult), sem="DV")
        p.wait("sync", "DV", dvn)
        st_c = p.dma("sync", OUT[tc * 128:(tc + 1) * 128, :], row[par][:, :], f"SS{2 + par}")
        p.set_key(("L2row", par), f"SS{2 + par}", st_c)


def emit_taps(p, t):
    if not t["tapped"]:
        return
    for s in ["ST", "SS0", "SS1", "SS2", "SS3"]:
        p.wait("sync", s, p.cnt[s])
    p.wait("sync", "DV", p.cnt["DV"])
    p.wait("sync", "SC", p.cnt["SC"])
    p.wait("sync", "PE", p.cnt["PE"])
    p.wait("sync", "GP", p.cnt["GP"])
    for name, dst in t["tapped"].items():
        if name in ("S_dump", "SB_dump"):
            continue
        p.dma("sync", dst, t["scr"][name], "ST")


def emit_final(p, t):
    for s in ["ST", "SS0", "SS1", "SS2", "SS3"]:
        p.wait("sync", s, p.cnt[s])
    p.wait("sync", "PE", p.cnt["PE"])
    p.wait("sync", "DV", p.cnt["DV"])
    p.wait("sync", "SC", p.cnt["SC"])
    p.wait("sync", "GP", p.cnt["GP"])
    p.wait("sync", "CC", p.cnt["CC"])
    p.wait("sync", "LD", p.cnt["LD"])
    p.wait("sync", "WL0", p.cnt["WL0"])
    p.wait("sync", "WL1", p.cnt["WL1"])
    p.wait("sync", "QL0", p.cnt["QL0"])
    p.wait("sync", "QL1", p.cnt["QL1"])
    p.wait("sync", "GL0", p.cnt["GL0"])
    p.wait("sync", "GL1", p.cnt["GL1"])
    p.wait("sync", "XL0", p.cnt["XL0"])
    p.wait("sync", "XL1", p.cnt["XL1"])



def kernel(**inputs):
    in_maps = pack_inputs(inputs)
    nc = build_program(taps=tuple(TAPS), stop_after=STOP_AFTER)
    res = run_bass_kernel_spmd(nc, in_maps, core_ids=list(range(NCORES)))
    out = np.concatenate([res.results[c]["out"] for c in range(NCORES)], axis=0)
    return out.reshape(BATCH, SEQ, D)
